# version 02
# baseline (speedup 1.0000x reference)
import numpy as np
import ml_dtypes
import concourse.bass as bass
import concourse.mybir as mybir
from concourse.bass_utils import run_bass_kernel_spmd

F32 = mybir.dt.float32
BF16 = mybir.dt.bfloat16
AF = mybir.ActivationFunctionType
ALU = mybir.AluOpType
AXL = mybir.AxisListType

P = 128
D = 1024
FH = 512
H = 8
DKH = 64
FF = 4096
CH = 128
ST = 2
EPS_LN = 1e-5
EPS_AT = 1e-6


class Eng:

    def __init__(self, name, no_drain=False):
        self.name = name
        self.no_drain = no_drain
        self.prog = []
        self.sem = None
        self.count = 0
        self.dirty_w = set()
        self.dirty_r = set()

    def drain(self):
        self.prog.append(lambda e: e.drain())
        self.dirty_w.clear()
        self.dirty_r.clear()

    def op(self, fn, reads=(), writes=(), inc=False):
        hazard = (not self.no_drain) and (
            any(t in self.dirty_w for t in reads) or
            any((t in self.dirty_w or t in self.dirty_r) for t in writes))
        if hazard:
            self.drain()
        if inc:
            self.count += 1
            me = self

            def run(e, fn=fn):
                fn(e).then_inc(me.sem, 1)
            self.prog.append(run)
        else:
            self.prog.append(lambda e, fn=fn: fn(e))
        self.dirty_r.update(reads)
        self.dirty_w.update(writes)
        return self.count

    def dma(self, dst_ap, src_ap, key="d"):
        if not hasattr(self, "dsem"):
            self.dsem = {}
            self.dcnt = {}
        if key not in self.dcnt:
            self.dcnt[key] = 0
            self.dsem[key] = None
        self.dcnt[key] += 1
        me = self

        def run(e, d=dst_ap, s=src_ap, key=key):
            e.dma_start(out=d, in_=s).then_inc(me.dsem[key], 16)
        self.prog.append(run)
        return (key, self.dcnt[key])

    def wait(self, other, value, dma=False):
        if dma:
            key, n = value
            if n <= 0:
                return
            me_other = other
            self.prog.append(lambda e: e.wait_ge(me_other.dsem[key], n * 16))
        else:
            if value <= 0:
                return
            me_other = other
            self.prog.append(lambda e: e.wait_ge(me_other.sem, value))


def build_program(T, debug=False, stage=5, bparts=5, cparts=5):
    nch = T // CH
    nst = nch // ST
    assert nch % ST == 0
    nc = bass.Bass("TRN2", target_bir_lowering=False)

    dt_ = nc.dram_tensor
    x_tok_d = dt_("x_tok", [nch, P, D], F32, kind="ExternalInput").ap()
    x_fmt_d = dt_("x_fmt", [nch, P, D], F32, kind="ExternalInput").ap()
    wq_d = dt_("wq", [P, 8 * FH], F32, kind="ExternalInput").ap()
    wk_d = dt_("wk", [P, 8 * FH], F32, kind="ExternalInput").ap()
    wv_d = dt_("wv", [P, 8 * FH], F32, kind="ExternalInput").ap()
    wg_d = dt_("wg", [P, 8 * FH], F32, kind="ExternalInput").ap()
    wo_d = dt_("wo", [P, 4 * D], F32, kind="ExternalInput").ap()
    fc_d = dt_("fc", [P, 8 * FF], F32, kind="ExternalInput").ap()
    pj_d = dt_("pj", [P, 32 * D], F32, kind="ExternalInput").ap()
    bg_d = dt_("bg", [P, 4], F32, kind="ExternalInput").ap()
    dl_d = dt_("dl", [1, H], F32, kind="ExternalInput").ap()
    identb_d = dt_("identb", [P, P], BF16, kind="ExternalInput").ap()
    identf_d = dt_("identf", [P, P], F32, kind="ExternalInput").ap()
    tsc_d = dt_("tsc", [P, P], BF16, kind="ExternalInput").ap()
    zmask_d = dt_("zmask", [P, P], BF16, kind="ExternalInput").ap()
    r1_d = dt_("r1", [P, P], BF16, kind="ExternalInput").ap()
    r127_d = dt_("r127", [P, 1], F32, kind="ExternalInput").ap()
    sel_d = dt_("sel", [H, FH], BF16, kind="ExternalInput").ap()
    onesr_d = dt_("onesr", [1, P], F32, kind="ExternalInput").ap()
    onescb_d = dt_("onescb", [P, 1], BF16, kind="ExternalInput").ap()
    out_d = dt_("out", [nch, P, D], F32, kind="ExternalOutput").ap()
    dbg = {}
    if debug:
        for nm, shp in [("dbg_xr", [P, 8 * ST * CH]), ("dbg_g", [P, ST * D]),
                        ("dbg_qf", [P, FH]), ("dbg_kf", [P, FH]), ("dbg_vt", [P, FH]),
                        ("dbg_th", [P, FH]), ("dbg_at", [P, H * P]), ("dbg_qg", [P, FH]),
                        ("dbg_ktok", [P, FH]), ("dbg_rec", [P, H]), ("dbg_yb", [P, FH]),
                        ("dbg_ssb", [P, 4 * 65]), ("dbg_vaug", [P, H * 65])]:
            dbg[nm] = dt_(nm, shp, F32, kind="ExternalOutput").ap()

    S_ = nc.alloc_sbuf_tensor
    wq = S_("wq_sb", [P, 8 * FH], BF16)
    wk = S_("wk_sb", [P, 8 * FH], BF16)
    wv = S_("wv_sb", [P, 8 * FH], BF16)
    wg = S_("wg_sb", [P, 8 * FH], BF16)
    wo = S_("wo_sb", [P, 4 * D], BF16)
    fc = S_("fc_sb", [P, 8 * FF], BF16)
    pj = S_("pj_sb", [P, 32 * D], BF16)
    XR = S_("XR", [P, 8 * ST * CH], BF16)
    XT = [S_(f"XT{i}", [P, D], BF16) for i in range(2)]
    XF = S_("XF", [P, D], BF16)
    G = S_("G", [P, ST * D], BF16)
    MEM = S_("MEM", [P, 2 * FH], F32)
    SGN = S_("SGN", [P, FH], BF16)
    QF = S_("QF", [P, FH], BF16)
    KF = S_("KF", [P, FH], BF16)
    THF = S_("THF", [P, FH], BF16)
    VT = S_("VT", [P, FH], BF16)
    VAUG = S_("VAUG", [P, H * 65], BF16)
    AT = S_("AT", [P, H * P], BF16)
    H2 = [S_(f"H2_{i}", [P, ST * CH], BF16) for i in range(2)]
    OUTB = MEM
    SSB = S_("SSB", [P, 4 * 65], BF16)
    GDEC = S_("GDEC", [P, 4 * 65], BF16)
    DMASK = S_("DMASK", [P, H * P], BF16)
    GQF = S_("GQF", [P, FH], BF16)
    KGM = S_("KGM", [P, FH], BF16)
    ZP = S_("ZP", [P, 2 * FH], BF16)
    SCRF = S_("SCRF", [P, FH], BF16)
    SCRB = S_("SCRB", [P, FH], BF16)
    RECT = SCRB[0:H, 0:P]
    identb = S_("identb_sb", [P, P], BF16)
    identf = S_("identf_sb", [P, P], F32)
    tsc = G[:, 0:P]
    zmask = G[:, P:2 * P]
    r1t = G[:, 2 * P:3 * P]
    r127 = S_("r127_sb", [P, 1], F32)
    sel = S_("sel_sb", [H, FH], BF16)
    onesr = S_("onesr_sb", [1, P], F32)
    onescb = S_("onescb_sb", [P, 1], BF16)
    bgh = S_("bgh", [P, 4], F32)
    dls = S_("dls", [1, H], F32)
    lgn = S_("lgn", [1, H], F32)
    NLGB = S_("NLGB", [P, H], F32)
    NLGF = S_("NLGF", [P, 4], F32)
    G128B = S_("G128B", [P, H], F32)
    KGC = S_("KGC", [P, H], F32)
    RBB = S_("RBB", [P, 2 * P], F32)
    RN0 = RBB[0:1, 0:P]
    RN1 = RBB[0:1, P:2 * P]
    REC = S_("REC", [P, H], F32)
    RECB = S_("RECB", [P, H], BF16)
    stats = S_("stats", [P, 16], F32)
    CONSTC = S_("CONSTC", [P, 3], F32)
    ONE64B = SCRB[:, 0:65]

    PS_ = nc.alloc_psum_tensor
    pb = [PS_(f"pb{i}", [P, FH], F32) for i in range(7)]
    pbT = PS_("pbT", [P, 2 * FH], BF16)

    E_pe = Eng("pe", no_drain=True)
    E_v, E_s, E_sy, E_gp = Eng("v"), Eng("s"), Eng("sy"), Eng("gp")

    def tap(name, buf_ap, prod, cnt):
        if not debug:
            return
        E_gp.wait(prod, cnt)
        kn = E_gp.dma(dbg[name], buf_ap, "tap")
        E_gp.wait(E_gp, kn, dma=True)
        for E in (E_v, E_s, E_pe):
            E.wait(E_gp, kn, dma=True)

    def mm(out_ap, lhsT_ap, rhs_ap, start, stop, reads=(), writes=(), inc=False, tp=None):
        def f(e):
            return e.matmul(out_ap, lhsT_ap, rhs_ap, start=start, stop=stop,
                            tile_position=tp)
        return E_pe.op(f, reads=reads, writes=writes, inc=inc)

    E_gp.dma(fc[:], fc_d, "wt")
    E_gp.dma(pj[:], pj_d, "wt")
    E_gp.dma(wq[:], wq_d, "wt")
    E_gp.dma(wk[:], wk_d, "wt")
    E_gp.dma(wv[:], wv_d, "wt")
    E_gp.dma(wg[:], wg_d, "wt")
    n_wo = E_gp.dma(wo[:], wo_d, "wt")

    for dst, src in [(identb[:], identb_d), (identf[:], identf_d), (tsc, tsc_d),
                     (zmask, zmask_d), (r1t, r1_d), (r127[:], r127_d),
                     (sel[:], sel_d), (onesr[:], onesr_d), (onescb[:], onescb_d),
                     (bgh[:], bg_d), (dls[:], dl_d)]:
        n_cst = E_sy.dma(dst, src)

    E_v.wait(E_sy, n_cst, dma=True)
    E_s.wait(E_sy, n_cst, dma=True)

    c = E_s.op(lambda e: e.activation(lgn[:], dls[:], AF.Exp, scale=-1.0),
               reads=("dls",), writes=("lgn",), inc=True)
    E_v.wait(E_s, c)
    c = E_v.op(lambda e: e.tensor_scalar_add(out=lgn[:], in0=lgn[:], scalar1=1.0),
               reads=("lgn",), writes=("lgn",), inc=True)
    E_s.wait(E_v, c)
    c_ln = E_s.op(lambda e: e.activation(lgn[:], lgn[:], AF.Ln),
                  reads=("lgn",), writes=("lgn",), inc=True)
    E_v.op(lambda e: e.tensor_scalar_mul(out=bgh[:], in0=bgh[:], scalar1=0.5),
           reads=("bgh",), writes=("bgh",))
    E_v.op(lambda e: e.memset(CONSTC[:, 0:1], -1.0), writes=("CONSTC",))
    E_v.op(lambda e: e.memset(CONSTC[:, 1:2], -6.0), writes=("CONSTC",))
    E_v.op(lambda e: e.memset(CONSTC[:, 2:3], 0.25), writes=("CONSTC",))
    E_v.op(lambda e: e.memset(ONE64B, 1.0), writes=("SCRB",))
    E_v.op(lambda e: e.memset(ZP[:], 0.0), writes=("ZP",))
    c_ssb0 = E_v.op(lambda e: e.memset(SSB[:], 0.0), writes=("SSB",), inc=True)

    E_pe.wait(E_s, c_ln)
    c = mm(pb[5][0:P, 0:H], onesr[:], lgn[:], True, True,
           reads=("onesr", "lgn"), writes=("pb5",), inc=True)
    E_v.wait(E_pe, c)
    c_nl = E_v.op(lambda e: e.tensor_scalar_mul(out=NLGB[:], in0=pb[5][0:P, 0:H],
                                                scalar1=-1.0),
                  reads=("pb5",), writes=("NLGB",), inc=True)
    E_s.wait(E_v, c_nl)
    c_g128 = E_s.op(lambda e: e.activation(G128B[:], NLGB[:], AF.Exp, scale=128.0),
                    reads=("NLGB",), writes=("G128B",), inc=True)
    for h in range(H):
        last_kgc = E_s.op(lambda e, h=h: e.activation(KGC[:, h:h + 1], r127[:], AF.Exp,
                                                      scale=NLGB[:, h:h + 1]),
                          reads=("r127", "NLGB"), writes=("KGC",),
                          inc=(h == H - 1))
    c_kgc = last_kgc
    for h in range(H):
        last_dm = E_s.op(lambda e, h=h: e.activation(DMASK[:, h * P:(h + 1) * P], tsc[:],
                                                     AF.Exp, scale=NLGB[:, h:h + 1]),
                         reads=("G", "NLGB"), writes=("DMASK",),
                         inc=(h == H - 1))
    c_dm = last_dm
    for ft in range(4):
        E_v.op(lambda e, ft=ft: e.tensor_copy(out=NLGF[0:DKH, ft:ft + 1],
                                              in_=NLGB[0:DKH, 2 * ft:2 * ft + 1]),
               reads=("NLGB",), writes=("NLGF",))
        last_nlgf = E_v.op(lambda e, ft=ft: e.tensor_copy(out=NLGF[DKH:P, ft:ft + 1],
                                                          in_=NLGB[DKH:P,
                                                                   2 * ft + 1:2 * ft + 2]),
                           reads=("NLGB",), writes=("NLGF",), inc=(ft == 3))
    E_s.wait(E_v, last_nlgf)
    for ft in range(4):
        last_gqf = E_s.op(lambda e, ft=ft: e.activation(GQF[:, ft * P:(ft + 1) * P], r1t[:],
                                                        AF.Exp, scale=NLGF[:, ft:ft + 1]),
                          reads=("G", "NLGF"), writes=("GQF",), inc=(ft == 3))
    E_v.wait(E_s, c_dm)
    for h in range(H):
        E_v.op(lambda e, h=h: e.tensor_tensor(out=DMASK[:, h * P:(h + 1) * P],
                                              in0=DMASK[:, h * P:(h + 1) * P],
                                              in1=zmask, op=ALU.mult),
               reads=("DMASK", "G"), writes=("DMASK",))
    E_v.wait(E_s, c_kgc)
    for h in range(H):
        E_v.op(lambda e, h=h: e.tensor_scalar_mul(out=KGM[:, h * DKH:(h + 1) * DKH],
                                                  in0=ONE64B[:, 0:DKH], scalar1=KGC[:, h:h + 1]),
               reads=("SCRB", "KGC"), writes=("KGM",))
        E_v.op(lambda e, h=h: e.tensor_copy(out=VAUG[:, h * 65 + 64:h * 65 + 65],
                                            in_=KGC[:, h:h + 1]),
               reads=("KGC",), writes=("VAUG",))
    E_v.wait(E_s, c_g128)
    for p_ in range(4):
        for r_ in range(2):
            h = 2 * p_ + r_
            E_v.op(lambda e, p_=p_, r_=r_, h=h: e.tensor_scalar_mul(
                out=GDEC[r_ * DKH:(r_ + 1) * DKH, p_ * 65:(p_ + 1) * 65],
                in0=ONE64B[r_ * DKH:(r_ + 1) * DKH, 0:65],
                scalar1=G128B[r_ * DKH:(r_ + 1) * DKH, h:h + 1]),
                reads=("SCRB", "G128B"), writes=("GDEC",))
    c_tables_v = E_v.op(lambda e: e.memset(stats[:], 0.0), writes=("stats",), inc=True)
    c_tables_s = last_gqf

    E_pe.wait(E_gp, ("wt", 7), dma=True)

    xt_cnt = {}
    xf_cnt = {}
    xt_cnt[0] = E_gp.dma(XT[0][:], x_tok_d[0], "xt0")
    xf_cnt[0] = E_gp.dma(XF[:], x_fmt_d[0], "xf")
    if nch > 1:
        xt_cnt[1] = E_gp.dma(XT[1][:], x_tok_d[1], "xt1")

    xnf_done = {-1: 0}
    combine_done = {-1: 0}
    s_state_done = {-1: c_ssb0}
    store_cnt = {-1: ('st', 0)}
    yb_done = {-1: 0}
    g_done = {}
    za_last = 0
    ymax_v = 0
    ymax_s = 0
    last_c_pr = c_tables_v
    relu31 = 0

    for sti in range(nst):
        c0 = sti * ST
        for cc in range(ST):
            ch = c0 + cc
            buf = XT[ch % 2]
            key = f"XT{ch % 2}"
            E_v.wait(E_gp, xt_cnt[ch], dma=True)
            E_v.op(lambda e, b=buf: e.reduce_sum(out=stats[:, 0:1], in_=b[:], axis=AXL.X),
                   reads=(key,), writes=("stats",))
            E_s.wait(E_gp, xt_cnt[ch], dma=True)
            E_s.wait(E_v, combine_done[c0 - 1])
            E_s.wait(E_v, last_c_pr)
            E_s.op(lambda e, b=buf: e.activation(SGN[:, 0:FH], b[:, 0:FH], AF.Square,
                                                 accum_out=stats[:, 1:2]),
                   reads=(key,), writes=("SGN", "stats_sq"))
            c_sq = E_s.op(lambda e, b=buf: e.activation(SGN[:, 0:FH], b[:, FH:D],
                                                        AF.Square,
                                                        accum_out=stats[:, 2:3]),
                          reads=(key,), writes=("SGN", "stats_sq"), inc=True)
            E_v.wait(E_s, c_sq)
            st_ = stats
            E_v.op(lambda e: e.tensor_scalar_mul(out=st_[:, 12:13], in0=st_[:, 0:1],
                                                 scalar1=1.0 / D),
                   reads=("stats",), writes=("stats",))
            E_v.op(lambda e: e.tensor_tensor(out=st_[:, 3:4], in0=st_[:, 1:2],
                                             in1=st_[:, 2:3], op=ALU.add),
                   reads=("stats_sq",), writes=("stats",))
            E_v.op(lambda e: e.tensor_scalar(out=st_[:, 3:4], in0=st_[:, 3:4],
                                             scalar1=1.0 / D, scalar2=EPS_LN,
                                             op0=ALU.mult, op1=ALU.add),
                   reads=("stats",), writes=("stats",))
            E_v.op(lambda e: e.tensor_tensor(out=st_[:, 4:5], in0=st_[:, 12:13],
                                             in1=st_[:, 12:13], op=ALU.mult),
                   reads=("stats",), writes=("stats",))
            E_v.op(lambda e: e.scalar_tensor_tensor(out=st_[:, 5:6], in0=st_[:, 4:5],
                                                    scalar=-1.0, in1=st_[:, 3:4],
                                                    op0=ALU.mult, op1=ALU.add),
                   reads=("stats",), writes=("stats",))
            E_v.op(lambda e: e.tensor_scalar(out=st_[:, 6:7], in0=st_[:, 5:6],
                                             scalar1=-0.5, scalar2=1.5,
                                             op0=ALU.mult, op1=ALU.add),
                   reads=("stats",), writes=("stats",))
            for _ in range(3):
                E_v.op(lambda e: e.tensor_tensor(out=st_[:, 7:8], in0=st_[:, 6:7],
                                                 in1=st_[:, 6:7], op=ALU.mult),
                       reads=("stats",), writes=("stats",))
                E_v.op(lambda e: e.tensor_tensor(out=st_[:, 7:8], in0=st_[:, 7:8],
                                                 in1=st_[:, 5:6], op=ALU.mult),
                       reads=("stats",), writes=("stats",))
                E_v.op(lambda e: e.tensor_scalar(out=st_[:, 7:8], in0=st_[:, 7:8],
                                                 scalar1=-0.5, scalar2=1.5,
                                                 op0=ALU.mult, op1=ALU.add),
                       reads=("stats",), writes=("stats",))
                E_v.op(lambda e: e.tensor_tensor(out=st_[:, 6:7], in0=st_[:, 6:7],
                                                 in1=st_[:, 7:8], op=ALU.mult),
                       reads=("stats",), writes=("stats",))
            E_v.op(lambda e: e.tensor_copy(out=st_[:, 9:10], in_=st_[:, 6:7]),
                   reads=("stats",), writes=("stats",))
            c_pr = E_v.op(lambda e: e.scalar_tensor_tensor(out=st_[:, 10:11],
                                                           in0=st_[:, 12:13], scalar=1.0,
                                                           in1=st_[:, 6:7],
                                                           op0=ALU.mult, op1=ALU.mult),
                          reads=("stats",), writes=("stats",), inc=True)
            last_c_pr = c_pr
            E_pe.wait(E_v, c_pr)
            E_pe.wait(E_v, combine_done[c0 - 1])
            mm(pb[5][0:1, 0:P], stats[:, 9:10], identf[:], True, True,
               reads=("stats", "identf"), writes=("pb5",))
            c_prT = mm(pb[5][0:1, P:2 * P], stats[:, 10:11], identf[:], True, True,
                       reads=("stats", "identf"), writes=("pb5",), inc=True)
            E_v.wait(E_pe, c_prT)
            E_v.op(lambda e: e.tensor_copy(out=RN0, in_=pb[5][0:1, 0:P]),
                   reads=("pb5",), writes=("RN",))
            c_rn = E_v.op(lambda e: e.tensor_copy(out=RN1, in_=pb[5][0:1, P:2 * P]),
                          reads=("pb5",), writes=("RN",), inc=True)
            E_pe.wait(E_v, c_rn)
            mm(pb[6][0:P, 0:P], onesr[:], RN0, True, True,
               reads=("onesr", "RN"), writes=("pb6",))
            c_rbb = mm(pb[6][0:P, P:2 * P], onesr[:], RN1, True, True,
                       reads=("onesr", "RN"), writes=("pb6",), inc=True)
            E_v.wait(E_pe, c_rbb)
            c_rbbE = E_v.op(lambda e: e.tensor_copy(out=RBB[:], in_=pb[6][0:P, 0:2 * P]),
                            reads=("pb6",), writes=("RBB",), inc=True)
            E_v.wait(E_gp, xf_cnt[ch], dma=True)
            for dk in range(8):
                col = dk * (ST * CH) + cc * CH
                E_v.op(lambda e, dk=dk, col=col: e.tensor_tensor(
                    out=XR[:, col:col + CH], in0=XF[:, dk * P:(dk + 1) * P],
                    in1=RBB[:, 0:P], op=ALU.mult),
                    reads=("XF", "RBB"), writes=("XR",))
            for dk in range(8):
                col = dk * (ST * CH) + cc * CH
                last_xn = E_v.op(lambda e, dk=dk, col=col: e.tensor_tensor(
                    out=XR[:, col:col + CH], in0=XR[:, col:col + CH],
                    in1=RBB[:, P:2 * P], op=ALU.subtract),
                    reads=("XR", "RBB"), writes=("XR",), inc=(dk == 7))
            xnf_done[ch] = last_xn
            if ch + 1 < nch:
                E_gp.wait(E_v, xnf_done[ch])
                xf_cnt[ch + 1] = E_gp.dma(XF[:], x_fmt_d[ch + 1], "xf")

        if sti == 0:
            tap("dbg_xr", XR[:], E_v, xnf_done[c0 + 1])
        if stage < 2:
            for cc in range(ST):
                ch = c0 + cc
                combine_done[ch] = E_v.count
                s_state_done[ch] = s_state_done.get(ch - 1, 0)
                yb_done[ch] = yb_done.get(ch - 1, 0)
                store_cnt[ch] = store_cnt[ch - 1]
                if ch + 2 < nch:
                    E_gp.wait(E_v, E_v.count)
                    xt_cnt[ch + 2] = E_gp.dma(XT[ch % 2][:], x_tok_d[ch + 2], f"xt{ch % 2}")
            continue
        relu_cnt = {}
        h2_cnt = {}
        pj_cnt = {}
        fc_cnt = {}

        def emit_fc(ft):
            bank = pb[ft % 2]
            E_pe.wait(E_s, relu_cnt.get(ft - 2, 0))
            if ft == 0:
                E_pe.wait(E_v, xnf_done[c0 + 1])
                E_pe.wait(E_v, combine_done[c0 - 1])
            for dk in range(8):
                cnt = mm(bank[:, 0:ST * CH],
                         fc[:, dk * FF + ft * P:dk * FF + (ft + 1) * P],
                         XR[:, dk * ST * CH:(dk + 1) * ST * CH],
                         start=(dk == 0), stop=(dk == 7),
                         reads=("fc_sb", "XR"), writes=(f"pb{ft % 2}",),
                         inc=(dk == 7))
            fc_cnt[ft] = cnt

        def emit_relu(ft):
            bank = pb[ft % 2]
            E_s.wait(E_pe, fc_cnt[ft])
            E_s.wait(E_pe, pj_cnt.get(ft - 2, 0))
            relu_cnt[ft] = E_s.op(
                lambda e, b=bank, ft=ft: e.activation(H2[ft % 2][:], b[:, 0:ST * CH],
                                                      AF.Relu),
                reads=(f"pb{ft % 2}",), writes=(f"H2_{ft % 2}",), inc=True)

        def emit_sq(ft):
            E_v.wait(E_s, relu_cnt[ft])
            h2_cnt[ft] = E_v.op(
                lambda e, ft=ft: e.tensor_tensor(out=H2[ft % 2][:], in0=H2[ft % 2][:],
                                                 in1=H2[ft % 2][:], op=ALU.mult),
                reads=(f"H2_{ft % 2}",), writes=(f"H2_{ft % 2}",), inc=True)

        def emit_pj(ft):
            if bparts < 2:
                pj_cnt[ft] = fc_cnt[ft]
                return
            E_pe.wait(E_v, h2_cnt[ft])
            if ft == 0:
                E_pe.wait(E_v, ymax_v)
                E_pe.wait(E_s, ymax_s)
            for tsub in range(ST):
                for doh in range(2):
                    bank = pb[2 + tsub * 2 + doh]
                    cnt = mm(bank[:, 0:FH],
                             H2[ft % 2][:, tsub * CH:(tsub + 1) * CH],
                             pj[:, ft * D + doh * FH:ft * D + (doh + 1) * FH],
                             start=(ft == 0), stop=(ft == 31),
                             reads=(f"H2_{ft % 2}", "pj_sb"),
                             writes=(f"pb{2 + tsub * 2 + doh}",),
                             inc=(tsub == ST - 1 and doh == 1))
            pj_cnt[ft] = cnt

        emit_fc(0)
        emit_relu(0)
        emit_sq(0)
        emit_fc(1)
        for ft in range(1, 32):
            emit_relu(ft)
            emit_sq(ft)
            emit_pj(ft - 1)
            if ft + 1 < 32:
                emit_fc(ft + 1)
        emit_pj(31)
        c_yready = pj_cnt[31]
        relu31 = relu_cnt[31]

        if bparts < 3:
            for tsub in range(ST):
                g_done[(sti, tsub)] = E_v.count
            if stage < 3:
                for cc in range(ST):
                    ch = c0 + cc
                    combine_done[ch] = E_v.count
                    s_state_done[ch] = s_state_done.get(ch - 1, 0)
                    yb_done[ch] = yb_done.get(ch - 1, 0)
                    store_cnt[ch] = store_cnt[ch - 1]
                    if ch + 2 < nch:
                        E_gp.wait(E_v, E_v.count)
                        xt_cnt[ch + 2] = E_gp.dma(XT[ch % 2][:], x_tok_d[ch + 2], f"xt{ch % 2}")
                continue_flag = True
            else:
                continue_flag = False
            if continue_flag:
                continue
        E_v.wait(E_sy, store_cnt[c0 - 1], dma=True)
        for tsub in range(ST):
            for doh in range(2):
                ybank = pb[2 + tsub * 2 + doh]
                ykey = f"pb{2 + tsub * 2 + doh}"
                MA = MEM[:, 0:FH]
                MB = MEM[:, FH:2 * FH]
                sg = SGN[:, 0:FH]
                E_s.wait(E_pe, c_yready)
                E_s.wait(E_v, za_last)
                sgn_c = E_s.op(lambda e, b=ybank: e.activation(sg, b[:], AF.Sign,
                                                               bias=CONSTC[:, 0:1]),
                               reads=(ykey, "CONSTC"), writes=("SGN",), inc=True)
                E_v.wait(E_pe, c_yready)
                E_v.wait(E_s, sgn_c)
                E_v.op(lambda e, b=ybank: e.tensor_scalar(out=MB, in0=b[:], scalar1=1.9,
                                                          scalar2=4.5, op0=ALU.mult,
                                                          op1=ALU.add),
                       reads=(ykey,), writes=("MEM",))
                if bparts >= 4:
                    E_pe.wait(E_s, sgn_c)
                    E_pe.wait(E_v, za_last)
                    acc_c = mm(pb[6][:, 0:FH], identb[:], sg, True, False,
                               reads=("identb", "SGN"), writes=("pb6",), inc=True)
                else:
                    acc_c = 0
                for n in range(2, 6):
                    E_v.wait(E_s, sgn_c)
                    mema_c = E_v.op(lambda e: e.scalar_tensor_tensor(
                        out=MA, in0=sg, scalar=-0.5, in1=MB, op0=ALU.mult, op1=ALU.add),
                        reads=("SGN", "MEM"), writes=("MEM",), inc=True)
                    if n < 5:
                        E_v.op(lambda e, b=ybank: e.scalar_tensor_tensor(
                            out=MB, in0=MA, scalar=0.9, in1=b[:], op0=ALU.mult,
                            op1=ALU.add),
                            reads=("MEM", ykey), writes=("MEM",))
                    E_s.wait(E_v, mema_c)
                    if bparts >= 4:
                        E_s.wait(E_pe, acc_c)
                    sgn_c = E_s.op(lambda e: e.activation(sg, MA, AF.Sign,
                                                          bias=CONSTC[:, 1:2]),
                                   reads=("MEM", "CONSTC"), writes=("SGN",), inc=True)
                    if bparts >= 4:
                        E_pe.wait(E_s, sgn_c)
                        acc_c = mm(pb[6][:, 0:FH], identb[:], sg, False, n == 5,
                                   reads=("identb", "SGN"), writes=("pb6",), inc=True)
                if bparts < 5:
                    za_last = E_v.op(lambda e: e.memset(G[:, 0:4], 0.0),
                                     writes=("G",), inc=True)
                    ymax_v = za_last
                    ymax_s = sgn_c
                    continue
                E_s.wait(E_pe, acc_c)
                thg_c = E_s.op(lambda e: e.activation(sg, pb[6][:, 0:FH], AF.Tanh,
                                                      scale=0.05, bias=CONSTC[:, 2:3]),
                              reads=("pb6",), writes=("SGN",), inc=True)
                E_v.wait(E_pe, acc_c)
                E_v.wait(E_s, thg_c)
                E_v.op(lambda e: e.tensor_scalar(out=MB, in0=pb[6][:, 0:FH], scalar1=0.05,
                                                 scalar2=0.25, op0=ALU.mult, op1=ALU.add),
                       reads=("pb6",), writes=("MEM",))
                E_v.wait(E_s, thg_c)
                gcol = tsub * D + doh * FH
                za_last = E_v.op(lambda e, gcol=gcol: e.scalar_tensor_tensor(
                    out=G[:, gcol:gcol + FH], in0=sg, scalar=1.0, in1=MB,
                    op0=ALU.add, op1=ALU.mult),
                    reads=("SGN", "MEM"), writes=("G",), inc=True)
                ymax_v = za_last
                ymax_s = thg_c
            g_done[(sti, tsub)] = za_last
        if sti == 0:
            tap("dbg_g", G[:], E_v, za_last)

        if stage < 3:
            for cc in range(ST):
                ch = c0 + cc
                combine_done[ch] = E_v.count
                s_state_done[ch] = s_state_done.get(ch - 1, 0)
                yb_done[ch] = yb_done.get(ch - 1, 0)
                store_cnt[ch] = store_cnt[ch - 1]
                if ch + 2 < nch:
                    E_gp.wait(E_v, E_v.count)
                    xt_cnt[ch + 2] = E_gp.dma(XT[ch % 2][:], x_tok_d[ch + 2], f"xt{ch % 2}")
            continue
        for cc in range(ST):
            ch = c0 + cc
            xcol = cc * CH

            E_pe.wait(E_v, combine_done[ch - 1])
            E_pe.wait(E_v, s_state_done[ch - 1])
            if cc == 0:
                E_pe.wait(E_s, ymax_s)
                E_pe.wait(E_v, ymax_v)
            E_s.wait(E_v, yb_done[ch - 1])
            for ft in range(4):
                for dk in range(8):
                    c_qps = mm(pb[0][:, ft * P:(ft + 1) * P],
                               wq[:, dk * FH + ft * P:dk * FH + (ft + 1) * P],
                               XR[:, dk * ST * CH + xcol:dk * ST * CH + xcol + CH],
                               start=(dk == 0), stop=(dk == 7),
                               reads=("wq_sb", "XR"), writes=("pb0",),
                               inc=(ft == 3 and dk == 7))
            E_s.wait(E_pe, c_qps)
            E_s.op(lambda e: e.activation(SCRF[:], pb[0][:, 0:FH], AF.Relu, scale=-1.0),
                   reads=("pb0",), writes=("SCRF",))
            c_qph = E_s.op(lambda e: e.activation(SCRB[:], SCRF[:], AF.Exp, scale=-1.0),
                           reads=("SCRF",), writes=("SCRB",), inc=True)
            E_v.wait(E_pe, c_qps)
            E_v.wait(E_s, c_qph)
            c_qf = E_v.op(lambda e: e.scalar_tensor_tensor(
                out=QF[:], in0=pb[0][:, 0:FH], scalar=0.0, in1=SCRB[:],
                op0=ALU.max, op1=ALU.add),
                reads=("pb0", "SCRB"), writes=("QF",), inc=True)
            if ch == 0:
                tap("dbg_qf", QF[:], E_v, c_qf)
            for ft in range(4):
                for dk in range(8):
                    c_kps = mm(pb[1][:, ft * P:(ft + 1) * P],
                               wk[:, dk * FH + ft * P:dk * FH + (ft + 1) * P],
                               XR[:, dk * ST * CH + xcol:dk * ST * CH + xcol + CH],
                               start=(dk == 0), stop=(dk == 7),
                               reads=("wk_sb", "XR"), writes=("pb1",),
                               inc=(ft == 3 and dk == 7))
            E_s.wait(E_pe, c_kps)
            E_s.wait(E_v, c_qf)
            E_s.op(lambda e: e.activation(SCRF[:], pb[1][:, 0:FH], AF.Relu, scale=-1.0),
                   reads=("pb1",), writes=("SCRF",))
            c_kph = E_s.op(lambda e: e.activation(SCRB[:], SCRF[:], AF.Exp, scale=-1.0),
                           reads=("SCRF",), writes=("SCRB",), inc=True)
            E_v.wait(E_pe, c_kps)
            E_v.wait(E_s, c_kph)
            c_kf = E_v.op(lambda e: e.scalar_tensor_tensor(
                out=KF[:], in0=pb[1][:, 0:FH], scalar=0.0, in1=SCRB[:],
                op0=ALU.max, op1=ALU.add),
                reads=("pb1", "SCRB"), writes=("KF",), inc=True)
            if ch == 0:
                tap("dbg_kf", KF[:], E_v, c_kf)
            E_pe.wait(E_s, c_qph)
            E_pe.wait(E_v, c_qf)
            for ft in range(4):
                for dk in range(8):
                    c_gps = mm(pb[0][:, ft * P:(ft + 1) * P],
                               wg[:, dk * FH + ft * P:dk * FH + (ft + 1) * P],
                               XR[:, dk * ST * CH + xcol:dk * ST * CH + xcol + CH],
                               start=(dk == 0), stop=(dk == 7),
                               reads=("wg_sb", "XR"), writes=("pb0",),
                               inc=(ft == 3 and dk == 7))
            E_s.wait(E_pe, c_gps)
            for ft in range(4):
                c_th = E_s.op(lambda e, ft=ft: e.activation(
                    THF[:, ft * P:(ft + 1) * P], pb[0][:, ft * P:(ft + 1) * P],
                    AF.Tanh, scale=0.5, bias=bgh[:, ft:ft + 1]),
                    reads=("pb0", "bgh"), writes=("THF",), inc=(ft == 3))
            if ch == 0:
                tap("dbg_th", THF[:], E_s, c_th)
            E_pe.wait(E_s, c_kph)
            E_pe.wait(E_v, c_kf)
            for dk in range(8):
                c_vps = mm(pb[1][:, 0:FH],
                           XR[:, dk * ST * CH + xcol:dk * ST * CH + xcol + CH],
                           wv[:, dk * FH:(dk + 1) * FH],
                           start=(dk == 0), stop=(dk == 7),
                           reads=("wv_sb", "XR"), writes=("pb1",), inc=(dk == 7))
            E_v.wait(E_pe, c_vps)
            E_v.op(lambda e: e.tensor_copy(out=VT[:], in_=pb[1][:, 0:FH]),
                   reads=("pb1",), writes=("VT",))
            vaug3 = VAUG[:].rearrange("p (h w) -> p h w", w=65)[:, :, 0:DKH]
            vt3 = VT[:].rearrange("p (h w) -> p h w", w=DKH)
            kg3 = KGM[:].rearrange("p (h w) -> p h w", w=DKH)
            c_vaug = E_v.op(lambda e: e.tensor_tensor(out=vaug3, in0=vt3, in1=kg3,
                                                      op=ALU.mult),
                            reads=("VT", "KGM"), writes=("VAUG",), inc=True)
            if ch == 0:
                tap("dbg_vt", VT[:], E_v, c_vaug)
                tap("dbg_vaug", VAUG[:], E_v, c_vaug)
            if stage < 4:
                combine_done[ch] = c_vaug
                s_state_done[ch] = s_state_done.get(ch - 1, 0)
                yb_done[ch] = yb_done.get(ch - 1, 0)
                store_cnt[ch] = store_cnt[ch - 1]
                if ch + 2 < nch:
                    E_gp.wait(E_v, E_v.count)
                    xt_cnt[ch + 2] = E_gp.dma(XT[ch % 2][:], x_tok_d[ch + 2], f"xt{ch % 2}")
                continue
            E_v.op(lambda e: e.tensor_copy(out=ZP[0:DKH, 0:FH], in_=KF[0:DKH, 0:FH]),
                   reads=("KF",), writes=("ZP",))
            c_kz = E_v.op(lambda e: e.tensor_copy(out=ZP[DKH:P, FH:2 * FH],
                                                  in_=KF[DKH:P, 0:FH]),
                          reads=("KF",), writes=("ZP",), inc=True)
            E_pe.wait(E_v, c_kz)
            for h in range(H):
                bank = pb[2 + h // 4]
                r_ = h % 2
                ft = h // 2
                c_amm = mm(bank[:, (h % 4) * P:(h % 4 + 1) * P],
                           ZP[:, r_ * FH + ft * P:r_ * FH + (ft + 1) * P],
                           QF[:, ft * P:(ft + 1) * P],
                           True, True,
                           reads=("ZP", "QF"), writes=(f"pb{2 + h // 4}",),
                           inc=False)
            for ft in range(4):
                def tr(e, ft=ft):
                    return e.transpose(pbT[:, ft * P:(ft + 1) * P],
                                       KF[:, ft * P:(ft + 1) * P], identb[:])
                c_amm = E_pe.op(tr, reads=("KF", "identb"), writes=("pbT",),
                                inc=(ft == 3))
            E_v.wait(E_pe, c_amm)
            E_v.op(lambda e: e.tensor_tensor(out=AT[:, 0:FH], in0=pb[2][:, 0:FH],
                                             in1=DMASK[:, 0:FH], op=ALU.mult),
                   reads=("pb2", "DMASK"), writes=("AT",))
            c_at = E_v.op(lambda e: e.tensor_tensor(out=AT[:, FH:2 * FH],
                                                    in0=pb[3][:, 0:FH],
                                                    in1=DMASK[:, FH:2 * FH],
                                                    op=ALU.mult),
                          reads=("pb3", "DMASK"), writes=("AT",), inc=True)
            E_v.op(lambda e: e.tensor_tensor(out=QF[:], in0=QF[:], in1=GQF[:],
                                               op=ALU.mult),
                   reads=("QF", "GQF"), writes=("QF",))
            E_v.op(lambda e: e.tensor_copy(out=ZP[0:DKH, 0:FH], in_=QF[0:DKH, 0:FH]),
                   reads=("QF",), writes=("ZP",))
            c_qg = E_v.op(lambda e: e.tensor_copy(out=ZP[DKH:P, FH:2 * FH],
                                                  in_=QF[DKH:P, 0:FH]),
                          reads=("QF",), writes=("ZP",), inc=True)
            if ch == 0:
                tap("dbg_at", AT[:], E_v, c_at)
                tap("dbg_qg", QF[:], E_v, c_qg)
            c_ktok = E_v.op(lambda e: e.tensor_copy(out=KF[:], in_=pbT[:, 0:FH]),
                            reads=("pbT",), writes=("KF",), inc=True)
            if ch == 0:
                tap("dbg_ktok", KF[:], E_v, c_ktok)
            if cparts < 2:
                combine_done[ch] = c_ktok
                s_state_done[ch] = s_state_done.get(ch - 1, 0)
                yb_done[ch] = yb_done.get(ch - 1, 0)
                store_cnt[ch] = store_cnt[ch - 1]
                if ch + 2 < nch:
                    E_gp.wait(E_v, E_v.count)
                    xt_cnt[ch + 2] = E_gp.dma(XT[ch % 2][:], x_tok_d[ch + 2], f"xt{ch % 2}")
                continue
            E_pe.wait(E_v, c_at)
            E_pe.wait(E_v, c_qg)
            for p_ in range(4):
                for r_ in range(2):
                    h = 2 * p_ + r_
                    mm(pb[4][r_ * DKH:(r_ + 1) * DKH, p_ * P:(p_ + 1) * P],
                       SSB[r_ * DKH:(r_ + 1) * DKH, p_ * 65:p_ * 65 + DKH],
                       QF[r_ * DKH:(r_ + 1) * DKH, p_ * P:(p_ + 1) * P],
                       True, False, reads=("SSB", "QF"), writes=("pb4",),
                       tp=(r_ * DKH, r_ * DKH))
                    mm(pb[4][r_ * DKH:(r_ + 1) * DKH, p_ * P:(p_ + 1) * P],
                       VT[:, h * DKH:(h + 1) * DKH],
                       AT[:, h * P:(h + 1) * P],
                       False, True, reads=("VT", "AT"), writes=("pb4",),
                       tp=(0, r_ * DKH))
            if cparts < 3:
                c_omm = E_pe.op(lambda e: e.matmul(pb[2][0:P, 0:P], identb[:],
                                                   AT[:, 0:P], start=True, stop=True),
                                reads=(), writes=("pb2d",), inc=True)
                E_v.wait(E_pe, c_omm)
                c_fake = E_v.op(lambda e: e.memset(REC[:], 1.0), writes=("REC",))
                c_rec = E_v.op(lambda e: e.tensor_copy(out=RECB[:], in_=REC[:]),
                               reads=("REC",), writes=("RECB",), inc=True)
                combine_done[ch] = c_rec
                s_state_done[ch] = s_state_done.get(ch - 1, 0)
                yb_done[ch] = yb_done.get(ch - 1, 0)
                store_cnt[ch] = store_cnt[ch - 1]
                if ch + 2 < nch:
                    E_gp.wait(E_v, E_v.count)
                    xt_cnt[ch + 2] = E_gp.dma(XT[ch % 2][:], x_tok_d[ch + 2], f"xt{ch % 2}")
                continue
            for h in range(H):
                p_, r_ = h // 2, h % 2
                mm(pb[2][0:P, h:h + 1], AT[:, h * P:(h + 1) * P], onescb[:],
                   True, False, reads=("AT", "onescb"), writes=("pb2d",))
                c_omm = mm(pb[2][0:P, h:h + 1],
                           QF[r_ * DKH:(r_ + 1) * DKH, p_ * P:(p_ + 1) * P],
                           SSB[r_ * DKH:(r_ + 1) * DKH, p_ * 65 + DKH:p_ * 65 + 65],
                           False, True, reads=("QF", "SSB"), writes=("pb2d",),
                           tp=(r_ * DKH, 0), inc=(h == H - 1))
            E_v.wait(E_pe, c_omm)
            E_v.op(lambda e: e.tensor_scalar_add(out=REC[:], in0=pb[2][0:P, 0:H],
                                                 scalar1=EPS_AT),
                   reads=("pb2d",), writes=("REC",))
            E_v.op(lambda e: e.reciprocal(out=REC[:], in_=REC[:]),
                   reads=("REC",), writes=("REC",))
            c_rec = E_v.op(lambda e: e.tensor_copy(out=RECB[:], in_=REC[:]),
                           reads=("REC",), writes=("RECB",), inc=True)
            if ch == 0:
                tap("dbg_rec", REC[:], E_v, c_rec)
            if cparts < 4:
                combine_done[ch] = c_rec
                s_state_done[ch] = s_state_done.get(ch - 1, 0)
                yb_done[ch] = yb_done.get(ch - 1, 0)
                store_cnt[ch] = store_cnt[ch - 1]
                if ch + 2 < nch:
                    E_gp.wait(E_v, E_v.count)
                    xt_cnt[ch + 2] = E_gp.dma(XT[ch % 2][:], x_tok_d[ch + 2], f"xt{ch % 2}")
                continue
            E_pe.wait(E_v, c_rec)
            c_rt = E_pe.op(lambda e: e.transpose(pbT[0:H, FH:FH + P], RECB[:], identb[:]),
                           reads=("RECB", "identb"), writes=("pbT2",), inc=True)
            E_v.wait(E_pe, c_rt)
            c_rte = E_v.op(lambda e: e.tensor_copy(out=RECT, in_=pbT[0:H, FH:FH + P]),
                           reads=("pbT2",), writes=("SCRB",), inc=True)
            E_pe.wait(E_v, c_rte)
            for ft in range(4):
                c_drec = mm(pb[3][:, ft * P:(ft + 1) * P], sel[:, ft * P:(ft + 1) * P],
                            RECT, True, True,
                            reads=("sel", "SCRB"), writes=("pb3d",), inc=(ft == 3))
            if cparts < 5:
                combine_done[ch] = c_rte
                s_state_done[ch] = s_state_done.get(ch - 1, 0)
                yb_done[ch] = yb_done.get(ch - 1, 0)
                store_cnt[ch] = store_cnt[ch - 1]
                if ch + 2 < nch:
                    E_gp.wait(E_v, E_v.count)
                    xt_cnt[ch + 2] = E_gp.dma(XT[ch % 2][:], x_tok_d[ch + 2], f"xt{ch % 2}")
                continue
            E_pe.wait(E_v, c_ktok)
            E_pe.wait(E_v, c_vaug)
            for p_ in range(4):
                for r_ in range(2):
                    h = 2 * p_ + r_
                    ft = h // 2
                    c_stmm = mm(pb[5][r_ * DKH:(r_ + 1) * DKH, p_ * 65:(p_ + 1) * 65],
                                KF[:, ft * P + r_ * DKH:ft * P + (r_ + 1) * DKH],
                                VAUG[:, h * 65:(h + 1) * 65],
                                True, True, reads=("KF", "VAUG"), writes=("pb5",),
                                tp=(0, r_ * DKH), inc=(h == H - 1))
            E_v.wait(E_pe, c_drec)
            E_v.wait(E_s, c_th)
            E_v.op(lambda e: e.scalar_tensor_tensor(out=SCRF[:], in0=THF[:], scalar=1.0,
                                                    in1=pb[4][:, 0:FH], op0=ALU.add,
                                                    op1=ALU.mult),
                   reads=("THF", "pb4"), writes=("SCRF",))
            c_yb = E_v.op(lambda e: e.scalar_tensor_tensor(out=SCRB[:], in0=SCRF[:],
                                                           scalar=0.5, in1=pb[3][:, 0:FH],
                                                           op0=ALU.mult, op1=ALU.mult),
                          reads=("SCRF", "pb3d"), writes=("SCRB",), inc=True)
            yb_done[ch] = c_yb
            if ch == 0:
                tap("dbg_yb", SCRB[:], E_v, c_yb)
            E_v.wait(E_pe, c_stmm)
            E_v.op(lambda e: e.tensor_tensor(out=SSB[:], in0=SSB[:], in1=GDEC[:],
                                             op=ALU.mult),
                   reads=("SSB", "GDEC"), writes=("SSB",))
            c_ssb = E_v.op(lambda e: e.tensor_tensor(out=SSB[:], in0=SSB[:],
                                                     in1=pb[5][:, 0:4 * 65], op=ALU.add),
                           reads=("SSB", "pb5"), writes=("SSB",), inc=True)
            s_state_done[ch] = c_ssb
            if ch == 0:
                tap("dbg_ssb", SSB[:], E_v, c_ssb)
            if stage < 5:
                combine_done[ch] = c_ssb
                store_cnt[ch] = store_cnt[ch - 1]
                if ch + 2 < nch:
                    E_gp.wait(E_v, E_v.count)
                    xt_cnt[ch + 2] = E_gp.dma(XT[ch % 2][:], x_tok_d[ch + 2], f"xt{ch % 2}")
                continue
            E_pe.wait(E_v, c_yb)
            for doh in range(2):
                for p_ in range(4):
                    c_wo = mm(pb[doh][:, 0:FH], SCRB[:, p_ * P:(p_ + 1) * P],
                              wo[:, p_ * D + doh * FH:p_ * D + (doh + 1) * FH],
                              start=(p_ == 0), stop=(p_ == 3),
                              reads=("SCRB", "wo_sb"), writes=(f"pb{doh}",),
                              inc=(doh == 1 and p_ == 3))
            E_v.wait(E_pe, c_wo)
            E_v.wait(E_sy, store_cnt[ch - 1], dma=True)
            for doh in range(2):
                bank = pb[doh]
                gcol = cc * D + doh * FH
                E_v.op(lambda e, b=bank, gcol=gcol, doh=doh: e.tensor_tensor(
                    out=OUTB[:, doh * FH:(doh + 1) * FH], in0=b[:, 0:FH],
                    in1=G[:, gcol:gcol + FH], op=ALU.mult),
                    reads=(f"pb{doh}", "G"), writes=("OUTB",))
            for doh in range(2):
                c_comb = E_v.op(lambda e, doh=doh, b=XT[ch % 2]: e.tensor_tensor(
                    out=OUTB[:, doh * FH:(doh + 1) * FH],
                    in0=OUTB[:, doh * FH:(doh + 1) * FH],
                    in1=b[:, doh * FH:(doh + 1) * FH], op=ALU.add),
                    reads=("OUTB", f"XT{ch % 2}"), writes=("OUTB",),
                    inc=(doh == 1))
            combine_done[ch] = c_comb
            E_sy.wait(E_v, c_comb)
            store_cnt[ch] = E_sy.dma(out_d[ch], OUTB[:, 0:D], "st")
            if ch + 2 < nch:
                E_gp.wait(E_v, c_comb)
                xt_cnt[ch + 2] = E_gp.dma(XT[ch % 2][:], x_tok_d[ch + 2], f"xt{ch % 2}")

    if stage >= 5:
        E_sy.wait(E_sy, store_cnt[nch - 1], dma=True)
    else:
        E_sy.wait(E_v, E_v.count)
        E_sy.wait(E_s, E_s.count)
        E_sy.wait(E_pe, E_pe.count)
        kn = E_sy.dma(out_d[0], OUTB[:, 0:D], "st")
        E_sy.wait(E_sy, kn, dma=True)

    for E in (E_pe, E_v, E_s, E_sy, E_gp):
        E.sem = nc.alloc_semaphore(f"sem_{E.name}")
        if hasattr(E, "dsem"):
            for key in E.dsem:
                E.dsem[key] = nc.alloc_semaphore(f"q_{E.name}_{key}")

    with nc.Block() as blk:
        @blk.sync
        def _(sync):
            for f in E_sy.prog:
                f(sync)

        @blk.gpsimd
        def _(g):
            for f in E_gp.prog:
                f(g)

        @blk.vector
        def _(v):
            for f in E_v.prog:
                f(v)

        @blk.scalar
        def _(s):
            for f in E_s.prog:
                f(s)

        @blk.tensor
        def _(t):
            for f in E_pe.prog:
                f(t)

    return nc


def _prep_host(inputs, T):
    nch = T // CH
    x = np.ascontiguousarray(inputs["L0_attn_x"])
    B = x.shape[0]
    WQ = inputs["L0_attn_W_Q"]
    WK = inputs["L0_attn_W_K"]
    WV = inputs["L0_attn_W_V"]
    WG = inputs["L0_attn_W_g_w"]
    WO = inputs["L0_attn_W_O"]
    BG = inputs["L0_attn_W_g_b"]
    DL = inputs["L0_attn_decay_log"]
    FC = inputs["L0_ffn_W_fc"]
    PJ = inputs["L0_ffn_W_proj"]

    def wlay(Wt, n):
        d, f = Wt.shape
        return np.ascontiguousarray(Wt.reshape(d // P, P, f).transpose(1, 0, 2)
                                    .reshape(P, (d // P) * f)).astype(np.float32)

    wq_h = wlay(np.ascontiguousarray(WQ.T), 8)
    wk_h = wlay(np.ascontiguousarray(WK.T), 8)
    wv_h = wlay(np.ascontiguousarray(WV.T), 8)
    wg_h = wlay(np.ascontiguousarray(WG.T), 8)
    wo_h = wlay(np.ascontiguousarray(WO.T), 4)
    fc_h = wlay(np.ascontiguousarray(FC.T), 8)
    pj_h = wlay(np.ascontiguousarray(PJ.T), 32)
    bg_h = np.ascontiguousarray(np.asarray(BG, np.float32).reshape(4, P).T)
    dl_h = np.ascontiguousarray(np.asarray(DL, np.float32).reshape(1, H))

    tt = np.arange(P)
    consts = {
        "identb": np.eye(P).astype(ml_dtypes.bfloat16),
        "identf": np.eye(P).astype(np.float32),
        "tsc": np.maximum(tt[None, :] - tt[:, None], 0).astype(ml_dtypes.bfloat16),
        "zmask": (tt[None, :] >= tt[:, None]).astype(ml_dtypes.bfloat16),
        "r1": np.broadcast_to(tt[None, :] + 1, (P, P)).astype(ml_dtypes.bfloat16),
        "r127": (127 - tt)[:, None].astype(np.float32),
        "onesr": np.ones((1, P), np.float32),
        "onescb": np.ones((P, 1), ml_dtypes.bfloat16),
    }
    selm = np.zeros((H, FH), ml_dtypes.bfloat16)
    for cfull in range(FH):
        selm[cfull // DKH, cfull] = 1.0
    consts["sel"] = selm

    in_maps = []
    for b in range(B):
        xb = x[b]
        x_tok = np.ascontiguousarray(xb.reshape(nch, P, D)).astype(np.float32)
        x_fmt = np.ascontiguousarray(
            xb.reshape(nch, P, 8, P).transpose(0, 3, 2, 1).reshape(nch, P, D)
        ).astype(np.float32)
        m = {"x_tok": x_tok, "x_fmt": x_fmt,
             "wq": wq_h, "wk": wk_h, "wv": wv_h, "wg": wg_h, "wo": wo_h,
             "fc": fc_h, "pj": pj_h, "bg": bg_h, "dl": dl_h}
        m.update(consts)
        in_maps.append(m)
    return in_maps


def kernel(**inputs):
    T = inputs["L0_attn_x"].shape[1]
    B = inputs["L0_attn_x"].shape[0]
    nch = T // CH
    nc = build_program(T)
    in_maps = _prep_host(inputs, T)
    res = run_bass_kernel_spmd(nc, in_maps, core_ids=list(range(B)))
    out = np.stack([res.results[b]["out"].reshape(T, D) for b in range(B)], axis=0)
    return out.astype(np.float32)


# version 03
# speedup vs baseline: 1.0419x; 1.0419x over previous
import numpy as np
import ml_dtypes
import concourse.bass as bass
import concourse.mybir as mybir
from concourse.bass_utils import run_bass_kernel_spmd

F32 = mybir.dt.float32
BF16 = mybir.dt.bfloat16
AF = mybir.ActivationFunctionType
ALU = mybir.AluOpType
AXL = mybir.AxisListType

P = 128
D = 1024
FH = 512
H = 8
DKH = 64
FF = 4096
CH = 128
ST = 2
EPS_LN = 1e-5
EPS_AT = 1e-6


class Eng:
    def __init__(self, name, no_drain=False):
        self.name = name
        self.no_drain = no_drain
        self.prog = []
        self.sem = None
        self.count = 0
        self.dirty_w = set()
        self.dirty_r = set()

    def drain(self):
        self.prog.append(lambda e: e.drain())
        self.dirty_w.clear()
        self.dirty_r.clear()

    def op(self, fn, reads=(), writes=(), inc=False):
        hazard = (not self.no_drain) and (
            any(t in self.dirty_w for t in reads) or
            any((t in self.dirty_w or t in self.dirty_r) for t in writes))
        if hazard:
            self.drain()
        if inc:
            self.count += 1
            me = self

            def run(e, fn=fn):
                fn(e).then_inc(me.sem, 1)
            self.prog.append(run)
        else:
            self.prog.append(lambda e, fn=fn: fn(e))
        self.dirty_r.update(reads)
        self.dirty_w.update(writes)
        return self.count

    def dma(self, dst_ap, src_ap, key="d"):
        if not hasattr(self, "dsem"):
            self.dsem = {}
            self.dcnt = {}
        if key not in self.dcnt:
            self.dcnt[key] = 0
            self.dsem[key] = None
        self.dcnt[key] += 1
        me = self

        def run(e, d=dst_ap, s=src_ap, key=key):
            e.dma_start(out=d, in_=s).then_inc(me.dsem[key], 16)
        self.prog.append(run)
        return (key, self.dcnt[key])

    def wait(self, other, value, dma=False):
        if dma:
            key, n = value
            if n <= 0:
                return
            o = other
            self.prog.append(lambda e: e.wait_ge(o.dsem[key], n * 16))
        else:
            if value <= 0:
                return
            o = other
            self.prog.append(lambda e: e.wait_ge(o.sem, value))


def build_program(T):
    nch = T // CH
    nst = nch // ST
    assert nch % ST == 0 and nch >= 4
    nc = bass.Bass("TRN2", target_bir_lowering=False)

    dt_ = nc.dram_tensor
    x_tok_d = dt_("x_tok", [nch, P, D], F32, kind="ExternalInput").ap()
    wq_d = dt_("wq", [P, 8 * FH], F32, kind="ExternalInput").ap()
    wk_d = dt_("wk", [P, 8 * FH], F32, kind="ExternalInput").ap()
    wv_d = dt_("wv", [P, 8 * FH], F32, kind="ExternalInput").ap()
    wg_d = dt_("wg", [P, 8 * FH], F32, kind="ExternalInput").ap()
    wo_d = dt_("wo", [P, 4 * D], F32, kind="ExternalInput").ap()
    fc_d = dt_("fc", [P, 8 * FF], F32, kind="ExternalInput").ap()
    pj_d = dt_("pj", [P, 32 * D], F32, kind="ExternalInput").ap()
    bg_d = dt_("bg", [P, 4], F32, kind="ExternalInput").ap()
    dl_d = dt_("dl", [1, H], F32, kind="ExternalInput").ap()
    identb_d = dt_("identb", [P, P], BF16, kind="ExternalInput").ap()
    tsc_d = dt_("tsc", [P, P], BF16, kind="ExternalInput").ap()
    zmask_d = dt_("zmask", [P, P], BF16, kind="ExternalInput").ap()
    r1_d = dt_("r1", [P, P], BF16, kind="ExternalInput").ap()
    r127_d = dt_("r127", [P, 1], F32, kind="ExternalInput").ap()
    sel_d = dt_("sel", [H, FH], BF16, kind="ExternalInput").ap()
    onesr_d = dt_("onesr", [1, P], F32, kind="ExternalInput").ap()
    onescb_d = dt_("onescb", [P, 1], BF16, kind="ExternalInput").ap()
    out_d = dt_("out", [nch, P, D], F32, kind="ExternalOutput").ap()

    S_ = nc.alloc_sbuf_tensor
    wq = S_("wq_sb", [P, 8 * FH], BF16)
    wk = S_("wk_sb", [P, 8 * FH], BF16)
    wv = S_("wv_sb", [P, 8 * FH], BF16)
    wg = S_("wg_sb", [P, 8 * FH], BF16)
    wo = S_("wo_sb", [P, 4 * D], BF16)
    fc = S_("fc_sb", [P, 8 * FF], BF16)
    pj = S_("pj_sb", [P, 32 * D], BF16)
    XR = S_("XR", [P, 8 * ST * CH], BF16)
    XT = [S_(f"XT{i}", [P, D], BF16) for i in range(2)]
    XF = S_("XF", [P, D], BF16)
    G = S_("G", [P, ST * D], BF16)
    MEM = S_("MEM", [P, D], F32)
    SGN = S_("SGN", [P, FH], BF16)
    QF = S_("QF", [P, FH], BF16)
    KF = S_("KF", [P, FH], BF16)
    THF = S_("THF", [P, FH], BF16)
    VT = S_("VT", [P, FH], BF16)
    VAUG = S_("VAUG", [P, H * 65], BF16)
    AT = S_("AT", [P, H * P], BF16)
    ZP = S_("ZP", [P, 2 * FH], BF16)
    SCRB = S_("SCRB", [P, FH], BF16)
    H2 = [S_(f"H2_{i}", [P, ST * CH], BF16) for i in range(2)]
    OUTB = MEM
    SSB = S_("SSB", [P, 4 * 65], BF16)
    GDEC = S_("GDEC", [P, 4 * 65], BF16)
    DMASK = S_("DMASK", [P, H * P], BF16)
    GQF = S_("GQF", [P, FH], BF16)
    KGM = S_("KGM", [P, FH], BF16)
    identb = S_("identb_sb", [P, P], BF16)
    tsc = G[:, 0:P]
    zmask = G[:, P:2 * P]
    r1t = G[:, 2 * P:3 * P]
    r127 = S_("r127_sb", [P, 1], F32)
    sel = S_("sel_sb", [H, FH], BF16)
    onesr = S_("onesr_sb", [1, P], F32)
    onescb = S_("onescb_sb", [P, 1], BF16)
    bgh = S_("bgh", [P, 4], F32)
    dls = S_("dls", [1, H], F32)
    lgn = S_("lgn", [1, H], F32)
    NLGB = S_("NLGB", [P, H], F32)
    NLGF = S_("NLGF", [P, 4], F32)
    G128B = S_("G128B", [P, H], F32)
    KGC = S_("KGC", [P, H], F32)
    REC = S_("REC", [P, H], F32)
    RECB = S_("RECB", [P, H], BF16)
    stats = S_("stats", [P, 16], F32)
    CONSTC = S_("CONSTC", [P, 3], F32)
    RECT = SCRB[0:H, 0:P]
    ONE64B = SCRB[:, 0:65]

    PS_ = nc.alloc_psum_tensor
    pb = [PS_(f"pb{i}", [P, FH], F32) for i in range(7)]
    pbT = PS_("pbT", [P, 2 * FH], BF16)

    E_pe = Eng("pe", no_drain=True)
    E_v, E_s, E_sy, E_gp = Eng("v"), Eng("s"), Eng("sy"), Eng("gp")

    def mm(out_ap, lhsT_ap, rhs_ap, start, stop, reads=(), writes=(), inc=False, tp=None):
        def f(e):
            return e.matmul(out_ap, lhsT_ap, rhs_ap, start=start, stop=stop,
                            tile_position=tp)
        return E_pe.op(f, reads=reads, writes=writes, inc=inc)

    for w_sb, w_dr in [(fc, fc_d), (pj, pj_d), (wq, wq_d), (wk, wk_d),
                       (wv, wv_d), (wg, wg_d)]:
        E_gp.dma(w_sb[:], w_dr, "wt")
    E_gp.dma(wo[:], wo_d, "wt")

    for dst, src in [(identb[:], identb_d), (tsc, tsc_d), (zmask, zmask_d),
                     (r1t, r1_d), (r127[:], r127_d), (sel[:], sel_d),
                     (onesr[:], onesr_d), (onescb[:], onescb_d),
                     (bgh[:], bg_d), (dls[:], dl_d)]:
        n_cst = E_sy.dma(dst, src, "cst")

    E_v.wait(E_sy, n_cst, dma=True)
    E_s.wait(E_sy, n_cst, dma=True)

    c = E_s.op(lambda e: e.activation(lgn[:], dls[:], AF.Exp, scale=-1.0),
               reads=("dls",), writes=("lgn",), inc=True)
    E_v.wait(E_s, c)
    c = E_v.op(lambda e: e.tensor_scalar_add(out=lgn[:], in0=lgn[:], scalar1=1.0),
               reads=("lgn",), writes=("lgn",), inc=True)
    E_s.wait(E_v, c)
    c_ln = E_s.op(lambda e: e.activation(lgn[:], lgn[:], AF.Ln),
                  reads=("lgn",), writes=("lgn",), inc=True)
    E_v.op(lambda e: e.tensor_scalar_mul(out=bgh[:], in0=bgh[:], scalar1=0.5),
           reads=("bgh",), writes=("bgh",))
    E_v.op(lambda e: e.memset(CONSTC[:, 0:1], -1.0), writes=("CONSTC",))
    E_v.op(lambda e: e.memset(CONSTC[:, 1:2], -6.0), writes=("CONSTC",))
    E_v.op(lambda e: e.memset(CONSTC[:, 2:3], 0.25), writes=("CONSTC",))
    E_v.op(lambda e: e.memset(ZP[:], 0.0), writes=("ZP",))
    E_v.op(lambda e: e.memset(ONE64B, 1.0), writes=("SCRB",))
    c_ssb0 = E_v.op(lambda e: e.memset(SSB[:], 0.0), writes=("SSB",), inc=True)

    E_pe.wait(E_s, c_ln)
    c = mm(pb[5][0:P, 0:H], onesr[:], lgn[:], True, True,
           reads=("onesr", "lgn"), writes=("pb5",), inc=True)
    E_v.wait(E_pe, c)
    c_nl = E_v.op(lambda e: e.tensor_scalar_mul(out=NLGB[:], in0=pb[5][0:P, 0:H],
                                                scalar1=-1.0),
                  reads=("pb5",), writes=("NLGB",), inc=True)
    E_s.wait(E_v, c_nl)
    c_g128 = E_s.op(lambda e: e.activation(G128B[:], NLGB[:], AF.Exp, scale=128.0),
                    reads=("NLGB",), writes=("G128B",), inc=True)
    for h in range(H):
        c_kgc = E_s.op(lambda e, h=h: e.activation(KGC[:, h:h + 1], r127[:], AF.Exp,
                                                   scale=NLGB[:, h:h + 1]),
                       reads=("r127", "NLGB"), writes=("KGC",), inc=(h == H - 1))
    for h in range(H):
        c_dm = E_s.op(lambda e, h=h: e.activation(DMASK[:, h * P:(h + 1) * P], tsc,
                                                  AF.Exp, scale=NLGB[:, h:h + 1]),
                      reads=("G", "NLGB"), writes=("DMASK",), inc=(h == H - 1))
    for ft in range(4):
        E_v.op(lambda e, ft=ft: e.tensor_copy(out=NLGF[0:DKH, ft:ft + 1],
                                              in_=NLGB[0:DKH, 2 * ft:2 * ft + 1]),
               reads=("NLGB",), writes=("NLGF",))
        c_nlgf = E_v.op(lambda e, ft=ft: e.tensor_copy(
            out=NLGF[DKH:P, ft:ft + 1], in_=NLGB[DKH:P, 2 * ft + 1:2 * ft + 2]),
            reads=("NLGB",), writes=("NLGF",), inc=(ft == 3))
    E_s.wait(E_v, c_nlgf)
    for ft in range(4):
        c_gqf = E_s.op(lambda e, ft=ft: e.activation(GQF[:, ft * P:(ft + 1) * P], r1t,
                                                     AF.Exp, scale=NLGF[:, ft:ft + 1]),
                       reads=("G", "NLGF"), writes=("GQF",), inc=(ft == 3))
    E_v.wait(E_s, c_dm)
    for h in range(H):
        E_v.op(lambda e, h=h: e.tensor_tensor(out=DMASK[:, h * P:(h + 1) * P],
                                              in0=DMASK[:, h * P:(h + 1) * P],
                                              in1=zmask, op=ALU.mult),
               reads=("DMASK", "G"), writes=("DMASK",))
    E_v.wait(E_s, c_kgc)
    for h in range(H):
        E_v.op(lambda e, h=h: e.tensor_scalar_mul(out=KGM[:, h * DKH:(h + 1) * DKH],
                                                  in0=ONE64B[:, 0:DKH],
                                                  scalar1=KGC[:, h:h + 1]),
               reads=("SCRB", "KGC"), writes=("KGM",))
        E_v.op(lambda e, h=h: e.tensor_copy(out=VAUG[:, h * 65 + 64:h * 65 + 65],
                                            in_=KGC[:, h:h + 1]),
               reads=("KGC",), writes=("VAUG",))
    E_v.wait(E_s, c_g128)
    for p_ in range(4):
        for r_ in range(2):
            h = 2 * p_ + r_
            E_v.op(lambda e, p_=p_, r_=r_, h=h: e.tensor_scalar_mul(
                out=GDEC[r_ * DKH:(r_ + 1) * DKH, p_ * 65:(p_ + 1) * 65],
                in0=ONE64B[r_ * DKH:(r_ + 1) * DKH, 0:65],
                scalar1=G128B[r_ * DKH:(r_ + 1) * DKH, h:h + 1]),
                reads=("SCRB", "G128B"), writes=("GDEC",), inc=(p_ == 3 and r_ == 1))

    E_pe.wait(E_gp, ("wt", 7), dma=True)
    E_pe.wait(E_v, c_ssb0)

    xt_cnt = {}
    xnact_done = {}
    xnte_done = {}
    xntp_done = {}
    gp_guard = {"s": 0, "g": 0}

    def load_xt(ch):
        E_gp.wait(E_s, xnact_done.get(ch - 2, 0))
        xt_cnt[ch] = E_gp.dma(XT[ch % 2][:], x_tok_d[ch], f"xt{ch % 2}")

    def emit_A(ch, cc):
        buf = XT[ch % 2]
        key = f"XT{ch % 2}"
        st_ = stats
        E_gp.wait(E_gp, xt_cnt[ch], dma=True)
        E_s.wait(E_gp, xt_cnt[ch], dma=True)
        E_s.wait(E_gp, gp_guard["g"])
        E_s.op(lambda e, b=buf: e.activation(XF[:], b[:], AF.Identity,
                                             accum_out=st_[:, 0:1]),
               reads=(key,), writes=("XF",))
        E_s.op(lambda e, b=buf: e.activation(XF[:, 0:FH], b[:, 0:FH], AF.Square,
                                             accum_out=st_[:, 1:2]),
               reads=(key,), writes=("XF",))
        c_sq = E_s.op(lambda e, b=buf: e.activation(XF[:, 0:FH], b[:, FH:D], AF.Square,
                                                    accum_out=st_[:, 2:3]),
                      reads=(key,), writes=("XF",), inc=True)
        E_gp.wait(E_s, c_sq)
        E_gp.wait(E_s, gp_guard["s"])
        g = E_gp
        g.op(lambda e: e.tensor_scalar_mul(out=st_[:, 12:13], in0=st_[:, 0:1],
                                           scalar1=1.0 / D),
             reads=("stats",), writes=("stats",))
        g.op(lambda e: e.tensor_scalar_mul(out=st_[:, 13:14], in0=st_[:, 0:1],
                                           scalar1=-1.0 / D),
             reads=("stats",), writes=("stats",))
        g.op(lambda e: e.tensor_tensor(out=st_[:, 3:4], in0=st_[:, 1:2],
                                       in1=st_[:, 2:3], op=ALU.add),
             reads=("stats",), writes=("stats",))
        g.op(lambda e: e.tensor_scalar(out=st_[:, 3:4], in0=st_[:, 3:4],
                                       scalar1=1.0 / D, scalar2=EPS_LN,
                                       op0=ALU.mult, op1=ALU.add),
             reads=("stats",), writes=("stats",))
        g.op(lambda e: e.tensor_tensor(out=st_[:, 4:5], in0=st_[:, 12:13],
                                       in1=st_[:, 13:14], op=ALU.mult),
             reads=("stats",), writes=("stats",))
        g.op(lambda e: e.tensor_tensor(out=st_[:, 5:6], in0=st_[:, 4:5],
                                       in1=st_[:, 3:4], op=ALU.add),
             reads=("stats",), writes=("stats",))
        g.op(lambda e: e.tensor_scalar(out=st_[:, 6:7], in0=st_[:, 5:6],
                                       scalar1=-0.5, scalar2=1.5,
                                       op0=ALU.mult, op1=ALU.add),
             reads=("stats",), writes=("stats",))
        for _ in range(3):
            g.op(lambda e: e.tensor_tensor(out=st_[:, 7:8], in0=st_[:, 6:7],
                                           in1=st_[:, 6:7], op=ALU.mult),
                 reads=("stats",), writes=("stats",))
            g.op(lambda e: e.tensor_tensor(out=st_[:, 7:8], in0=st_[:, 7:8],
                                           in1=st_[:, 5:6], op=ALU.mult),
                 reads=("stats",), writes=("stats",))
            g.op(lambda e: e.tensor_scalar(out=st_[:, 7:8], in0=st_[:, 7:8],
                                           scalar1=-0.5, scalar2=1.5,
                                           op0=ALU.mult, op1=ALU.add),
                 reads=("stats",), writes=("stats",))
            g.op(lambda e: e.tensor_tensor(out=st_[:, 6:7], in0=st_[:, 6:7],
                                           in1=st_[:, 7:8], op=ALU.mult),
                 reads=("stats",), writes=("stats",))
        g.op(lambda e: e.tensor_tensor(out=st_[:, 10:11], in0=st_[:, 12:13],
                                       in1=st_[:, 6:7], op=ALU.mult),
             reads=("stats",), writes=("stats",))
        c_neg = g.op(lambda e: e.tensor_scalar_mul(out=st_[:, 11:12],
                                                   in0=st_[:, 10:11], scalar1=-1.0),
                     reads=("stats",), writes=("stats",), inc=True)
        gp_guard["g"] = c_neg
        E_s.wait(E_gp, c_neg)
        c_xn = E_s.op(lambda e, b=buf: e.activation(XF[:], b[:], AF.Identity,
                                                    scale=st_[:, 6:7],
                                                    bias=st_[:, 11:12]),
                      reads=(key, "stats"), writes=("XF",), inc=True)
        xnact_done[ch] = c_xn
        gp_guard["s"] = c_xn
        E_pe.wait(E_s, c_xn)
        for dk in range(8):
            c_T = E_pe.op(lambda e, dk=dk: e.transpose(pbT[:, dk * P:(dk + 1) * P],
                                                       XF[:, dk * P:(dk + 1) * P],
                                                       identb[:]),
                          reads=("XF", "identb"), writes=("pbT",), inc=(dk == 7))
        xntp_done[ch] = c_T
        E_s.wait(E_pe, c_T)
        for dk in range(8):
            col = dk * (ST * CH) + cc * CH
            c_ev = E_s.op(lambda e, dk=dk, col=col: e.activation(
                XR[:, col:col + CH], pbT[:, dk * P:(dk + 1) * P], AF.Identity),
                reads=("pbT",), writes=("XR",), inc=(dk == 7))
        xnte_done[ch] = c_ev

    load_xt(0)
    load_xt(1)
    emit_A(0, 0)
    emit_A(1, 1)

    combine_done = {-1: 0, -2: 0}
    store_cnt = {-1: ("st", 0)}
    c_vtab = E_v.count
    th_done = {}
    vaug_done = {}
    ssb_done = {-1: c_ssb0}
    yb_done = {-1: c_vtab, -2: 0}
    wo_done = {-1: 0}
    prev_xnte = xnte_done[1]
    za_pair = 0
    yq_free_v = 0
    yq_free_s = 0

    for sti in range(nst):
        c0 = sti * ST
        chunks = [c0, c0 + 1]
        for cc, ch in enumerate(chunks):
            qbank, kbank = pb[2], pb[3]
            if cc == 0:
                E_pe.wait(E_v, yq_free_v)
                E_pe.wait(E_s, yq_free_s)
                E_pe.wait(E_v, combine_done[c0 - 1])
                E_pe.wait(E_s, prev_xnte)
            else:
                E_pe.wait(E_s, th_done[ch - 1])
                E_pe.wait(E_v, vaug_done[ch - 1])
            for ft in range(4):
                for dk in range(8):
                    c_qps = mm(qbank[:, ft * P:(ft + 1) * P],
                               wq[:, dk * FH + ft * P:dk * FH + (ft + 1) * P],
                               XR[:, dk * ST * CH + cc * CH:dk * ST * CH + cc * CH + CH],
                               start=(dk == 0), stop=(dk == 7),
                               reads=("wq_sb", "XR"), writes=("pb2",),
                               inc=(ft == 3 and dk == 7))
            E_s.wait(E_pe, c_qps)
            E_s.wait(E_pe, wo_done.get(ch - 1, 0))
            E_s.wait(E_v, yb_done.get(ch - 1, 0))
            E_s.op(lambda e: e.activation(QF[:], qbank[:, 0:FH], AF.Relu, scale=-1.0),
                   reads=("pb2",), writes=("QF",))
            c_qph = E_s.op(lambda e: e.activation(SCRB[:], QF[:], AF.Exp, scale=-1.0),
                           reads=("QF",), writes=("SCRB",), inc=True)
            E_v.wait(E_pe, c_qps)
            E_v.wait(E_s, c_qph)
            c_qf = E_v.op(lambda e: e.scalar_tensor_tensor(
                out=QF[:], in0=qbank[:, 0:FH], scalar=0.0, in1=SCRB[:],
                op0=ALU.max, op1=ALU.add),
                reads=("pb2", "SCRB"), writes=("QF",), inc=True)
            for ft in range(4):
                for dk in range(8):
                    c_kps = mm(kbank[:, ft * P:(ft + 1) * P],
                               wk[:, dk * FH + ft * P:dk * FH + (ft + 1) * P],
                               XR[:, dk * ST * CH + cc * CH:dk * ST * CH + cc * CH + CH],
                               start=(dk == 0), stop=(dk == 7),
                               reads=("wk_sb", "XR"), writes=("pb3",),
                               inc=(ft == 3 and dk == 7))
            E_s.wait(E_pe, c_kps)
            E_s.wait(E_v, c_qf)
            E_s.op(lambda e: e.activation(KF[:], kbank[:, 0:FH], AF.Relu, scale=-1.0),
                   reads=("pb3",), writes=("KF",))
            c_kph = E_s.op(lambda e: e.activation(SCRB[:], KF[:], AF.Exp, scale=-1.0),
                           reads=("KF",), writes=("SCRB",), inc=True)
            E_v.wait(E_pe, c_kps)
            E_v.wait(E_s, c_kph)
            c_kf = E_v.op(lambda e: e.scalar_tensor_tensor(
                out=KF[:], in0=kbank[:, 0:FH], scalar=0.0, in1=SCRB[:],
                op0=ALU.max, op1=ALU.add),
                reads=("pb3", "SCRB"), writes=("KF",), inc=True)
            E_v.op(lambda e: e.tensor_copy(out=ZP[0:DKH, 0:FH], in_=KF[0:DKH, 0:FH]),
                   reads=("KF",), writes=("ZP",))
            c_kz = E_v.op(lambda e: e.tensor_copy(out=ZP[DKH:P, FH:2 * FH],
                                                  in_=KF[DKH:P, 0:FH]),
                          reads=("KF",), writes=("ZP",), inc=True)
            E_pe.wait(E_s, c_qph)
            E_pe.wait(E_v, c_qf)
            for ft in range(4):
                for dk in range(8):
                    c_gps = mm(qbank[:, ft * P:(ft + 1) * P],
                               wg[:, dk * FH + ft * P:dk * FH + (ft + 1) * P],
                               XR[:, dk * ST * CH + cc * CH:dk * ST * CH + cc * CH + CH],
                               start=(dk == 0), stop=(dk == 7),
                               reads=("wg_sb", "XR"), writes=("pb2",),
                               inc=(ft == 3 and dk == 7))
            E_s.wait(E_pe, c_gps)
            E_s.wait(E_v, yb_done.get(ch - 1, 0))
            for ft in range(4):
                c_th = E_s.op(lambda e, ft=ft: e.activation(
                    THF[:, ft * P:(ft + 1) * P], qbank[:, ft * P:(ft + 1) * P],
                    AF.Tanh, scale=0.5, bias=bgh[:, ft:ft + 1]),
                    reads=("pb2", "bgh"), writes=("THF",), inc=(ft == 3))
            th_done[ch] = c_th
            E_pe.wait(E_s, c_kph)
            E_pe.wait(E_v, c_kf)
            for dk in range(8):
                c_vps = mm(kbank[:, 0:FH],
                           XR[:, dk * ST * CH + cc * CH:dk * ST * CH + cc * CH + CH],
                           wv[:, dk * FH:(dk + 1) * FH],
                           start=(dk == 0), stop=(dk == 7),
                           reads=("wv_sb", "XR"), writes=("pb3",), inc=(dk == 7))
            E_v.wait(E_pe, c_vps)
            E_v.op(lambda e: e.tensor_copy(out=VT[:], in_=kbank[:, 0:FH]),
                   reads=("pb3",), writes=("VT",))
            vaug3 = VAUG[:].rearrange("p (h w) -> p h w", w=65)[:, :, 0:DKH]
            vt3 = VT[:].rearrange("p (h w) -> p h w", w=DKH)
            kg3 = KGM[:].rearrange("p (h w) -> p h w", w=DKH)
            c_vaug = E_v.op(lambda e: e.tensor_tensor(out=vaug3, in0=vt3, in1=kg3,
                                                      op=ALU.mult),
                            reads=("VT", "KGM"), writes=("VAUG",), inc=True)
            vaug_done[ch] = c_vaug
            E_pe.wait(E_v, c_kz)
            E_pe.wait(E_v, ssb_done.get(ch - 1, 0))
            for h in range(H):
                r_, ft = h % 2, h // 2
                abank = pb[h // 4]
                c_amm = mm(abank[:, (h % 4) * P:(h % 4 + 1) * P],
                           ZP[:, r_ * FH + ft * P:r_ * FH + (ft + 1) * P],
                           QF[:, ft * P:(ft + 1) * P],
                           True, True, reads=("ZP", "QF"), writes=(f"pb{h // 4}",),
                           inc=(h == H - 1))
            E_v.wait(E_pe, c_amm)
            E_v.op(lambda e: e.tensor_tensor(out=AT[:, 0:FH], in0=pb[0][:, 0:FH],
                                             in1=DMASK[:, 0:FH], op=ALU.mult),
                   reads=("pb0", "DMASK"), writes=("AT",))
            c_at = E_v.op(lambda e: e.tensor_tensor(out=AT[:, FH:2 * FH],
                                                    in0=pb[1][:, 0:FH],
                                                    in1=DMASK[:, FH:2 * FH],
                                                    op=ALU.mult),
                          reads=("pb1", "DMASK"), writes=("AT",), inc=True)
            E_v.op(lambda e: e.tensor_tensor(out=QF[:], in0=QF[:], in1=GQF[:],
                                             op=ALU.mult),
                   reads=("QF", "GQF"), writes=("QF",))
            E_v.op(lambda e: e.tensor_copy(out=ZP[0:DKH, 0:FH], in_=QF[0:DKH, 0:FH]),
                   reads=("QF",), writes=("ZP",))
            c_qz = E_v.op(lambda e: e.tensor_copy(out=ZP[DKH:P, FH:2 * FH],
                                                  in_=QF[DKH:P, 0:FH]),
                          reads=("QF",), writes=("ZP",), inc=True)
            E_pe.wait(E_v, c_at)
            for ft in range(4):
                c_kt = E_pe.op(lambda e, ft=ft: e.transpose(
                    pbT[:, ft * P:(ft + 1) * P], KF[:, ft * P:(ft + 1) * P],
                    identb[:]),
                    reads=("KF", "identb"), writes=("pbT",), inc=(ft == 3))
            opb = pb[4] if cc == 0 else pb[5]
            opkey = "pb4" if cc == 0 else "pb5"
            E_pe.wait(E_v, c_qz)
            for p_ in range(4):
                for r_ in range(2):
                    h = 2 * p_ + r_
                    mm(opb[r_ * DKH:(r_ + 1) * DKH, p_ * P:(p_ + 1) * P],
                       SSB[:, p_ * 65:p_ * 65 + DKH],
                       ZP[:, r_ * FH + p_ * P:r_ * FH + (p_ + 1) * P],
                       True, False, reads=("SSB", "ZP"), writes=(opkey,),
                       tp=(0, r_ * DKH))
                    mm(opb[r_ * DKH:(r_ + 1) * DKH, p_ * P:(p_ + 1) * P],
                       VT[:, h * DKH:(h + 1) * DKH],
                       AT[:, h * P:(h + 1) * P],
                       False, True, reads=("VT", "AT"), writes=(opkey,),
                       tp=(0, r_ * DKH))
            for h in range(H):
                p_, r_ = h // 2, h % 2
                mm(pb[0][0:P, h:h + 1], AT[:, h * P:(h + 1) * P], onescb[:],
                   True, False, reads=("AT", "onescb"), writes=("pb0",))
                c_omm = mm(pb[0][0:P, h:h + 1],
                           ZP[:, r_ * FH + p_ * P:r_ * FH + (p_ + 1) * P],
                           SSB[:, p_ * 65 + DKH:p_ * 65 + 65],
                           False, True, reads=("ZP", "SSB"), writes=("pb0",),
                           inc=(h == H - 1))
            E_v.wait(E_pe, c_kt)
            c_ktok = E_v.op(lambda e: e.tensor_copy(out=KF[:], in_=pbT[:, 0:FH]),
                            reads=("pbT",), writes=("KF",), inc=True)
            E_v.wait(E_pe, c_omm)
            E_v.op(lambda e: e.tensor_scalar_add(out=REC[:], in0=pb[0][0:P, 0:H],
                                                 scalar1=EPS_AT),
                   reads=("pb0",), writes=("REC",))
            E_v.op(lambda e: e.reciprocal(out=REC[:], in_=REC[:]),
                   reads=("REC",), writes=("REC",))
            c_rec = E_v.op(lambda e: e.tensor_copy(out=RECB[:], in_=REC[:]),
                           reads=("REC",), writes=("RECB",), inc=True)
            E_pe.wait(E_v, c_rec)
            c_rt = E_pe.op(lambda e: e.transpose(pbT[0:H, FH:FH + P], RECB[:],
                                                 identb[:]),
                           reads=("RECB", "identb"), writes=("pbT",), inc=True)
            E_v.wait(E_pe, c_rt)
            c_rte = E_v.op(lambda e: e.tensor_copy(out=RECT, in_=pbT[0:H, FH:FH + P]),
                           reads=("pbT",), writes=("SCRB",), inc=True)
            E_pe.wait(E_v, c_rte)
            for ft in range(4):
                mm(pb[1][:, ft * P:(ft + 1) * P], sel[:, ft * P:(ft + 1) * P],
                   RECT, True, True, reads=("sel", "SCRB"), writes=("pb1",))
            for p_ in range(4):
                for r_ in range(2):
                    h = 2 * p_ + r_
                    ft = h // 2
                    c_stmm = mm(pb[0][r_ * DKH:(r_ + 1) * DKH,
                                      128 + p_ * 65:128 + (p_ + 1) * 65],
                                KF[:, ft * P + r_ * DKH:ft * P + (r_ + 1) * DKH],
                                VAUG[:, h * 65:(h + 1) * 65],
                                True, True, reads=("KF", "VAUG"), writes=("pb0",),
                                tp=(0, r_ * DKH), inc=(h == H - 1))
            ybbuf = SGN if cc == 0 else SCRB
            ybkey = "SGN" if cc == 0 else "SCRB"
            E_v.wait(E_pe, c_stmm)
            E_v.wait(E_s, c_th)
            if cc == 0:
                E_v.wait(E_pe, wo_done.get(ch - 2, 0))
            E_v.op(lambda e, b=opb: e.scalar_tensor_tensor(
                out=QF[:], in0=THF[:], scalar=1.0, in1=b[:, 0:FH],
                op0=ALU.add, op1=ALU.mult),
                reads=("THF", opkey), writes=("QF",))
            c_yb = E_v.op(lambda e, yb_=ybbuf: e.scalar_tensor_tensor(
                out=yb_[:, 0:FH], in0=QF[:], scalar=0.5, in1=pb[1][:, 0:FH],
                op0=ALU.mult, op1=ALU.mult),
                reads=("QF", "pb1"), writes=(ybkey,), inc=True)
            yb_done[ch] = c_yb
            E_v.op(lambda e: e.tensor_tensor(out=SSB[:], in0=SSB[:], in1=GDEC[:],
                                             op=ALU.mult),
                   reads=("SSB", "GDEC"), writes=("SSB",))
            c_ssb = E_v.op(lambda e: e.tensor_tensor(out=SSB[:], in0=SSB[:],
                                                     in1=pb[0][:, 128:388], op=ALU.add),
                           reads=("SSB", "pb0"), writes=("SSB",), inc=True)
            ssb_done[ch] = c_ssb

        relu_cnt = {}
        h2_cnt = {}
        fc_cnt = {}
        pj_cnt = {}
        ymap = {(0, 0): 4, (0, 1): 5, (1, 0): 0, (1, 1): 1}
        th_c1 = c_th
        vaug_c1 = c_vaug

        def emit_fc(ft):
            bank = pb[2 + ft % 2]
            E_pe.wait(E_s, relu_cnt.get(ft - 2, 0))
            if ft == 0:
                E_pe.wait(E_s, th_c1)
                E_pe.wait(E_v, vaug_c1)
            for dk in range(8):
                c_fc = mm(bank[:, 0:ST * CH],
                          fc[:, dk * FF + ft * P:dk * FF + (ft + 1) * P],
                          XR[:, dk * ST * CH:(dk + 1) * ST * CH],
                          start=(dk == 0), stop=(dk == 7),
                          reads=("fc_sb", "XR"), writes=(f"pb{2 + ft % 2}",),
                          inc=(dk == 7))
            fc_cnt[ft] = c_fc
            E_s.wait(E_pe, c_fc)
            E_s.wait(E_pe, pj_cnt.get(ft - 2, 0))
            relu_cnt[ft] = E_s.op(
                lambda e, b=bank, ft=ft: e.activation(H2[ft % 2][:], b[:, 0:ST * CH],
                                                      AF.Relu),
                reads=(f"pb{2 + ft % 2}",), writes=(f"H2_{ft % 2}",), inc=True)
            E_v.wait(E_s, relu_cnt[ft])
            h2_cnt[ft] = E_v.op(
                lambda e, ft=ft: e.tensor_tensor(out=H2[ft % 2][:], in0=H2[ft % 2][:],
                                                 in1=H2[ft % 2][:], op=ALU.mult),
                reads=(f"H2_{ft % 2}",), writes=(f"H2_{ft % 2}",), inc=True)

        def emit_pj(ft):
            E_pe.wait(E_v, h2_cnt[ft])
            if ft == 0:
                E_pe.wait(E_v, ssb_done[c0 + 1])
            for tsub in range(ST):
                for doh in range(2):
                    bank = pb[ymap[(tsub, doh)]]
                    c_pj = mm(bank[:, 0:FH],
                              H2[ft % 2][:, tsub * CH:(tsub + 1) * CH],
                              pj[:, ft * D + doh * FH:ft * D + (doh + 1) * FH],
                              start=(ft == 0), stop=(ft == 31),
                              reads=(f"H2_{ft % 2}", "pj_sb"),
                              writes=(f"pb{ymap[(tsub, doh)]}",),
                              inc=(tsub == ST - 1 and doh == 1))
            pj_cnt[ft] = c_pj

        emit_fc(0)
        emit_fc(1)
        for ft in range(1, 32):
            emit_pj(ft - 1)
            if ft + 1 < 32:
                emit_fc(ft + 1)
        emit_pj(31)
        c_yready = pj_cnt[31]
        relu31 = relu_cnt[31]

        if sti + 1 < nst:
            load_xt(c0 + 2)
            load_xt(c0 + 3)
            E_pe.wait(E_v, ssb_done[c0 + 1])
            E_pe.wait(E_v, yb_done[c0 + 1])
            emit_A(c0 + 2, 0)
            emit_A(c0 + 3, 1)
            prev_xnte = xnte_done[c0 + 3]

        E_pe.wait(E_v, yb_done[c0])
        E_pe.wait(E_s, relu31)
        for doh in range(2):
            for p_ in range(4):
                c_wo0 = mm(pb[2 + doh][:, 0:FH], SGN[:, p_ * P:(p_ + 1) * P],
                           wo[:, p_ * D + doh * FH:p_ * D + (doh + 1) * FH],
                           start=(p_ == 0), stop=(p_ == 3),
                           reads=("SGN", "wo_sb"), writes=(f"pb{2 + doh}",),
                           inc=(doh == 1 and p_ == 3))
        wo_done[c0] = c_wo0

        inv = {v: k for k, v in ymap.items()}
        qlist = []
        for b_ in (4, 5, 0, 1):
            for sub in range(2):
                qlist.append((b_, sub))
        sgn_cnt = {}
        E_v.wait(E_pe, c_yready)
        E_v.wait(E_sy, store_cnt.get(c0 - 1, ("st", 0)), dma=True)

        def emit_pair(pi):
            global_unused = None
            nonlocal_vars = None
            return None

        def spike_pair(pi, za_pair):
            ctxs = [(qlist[2 * pi], 0), (qlist[2 * pi + 1], 1)]
            for (bk, sub), cx in ctxs:
                E_s.wait(E_pe, c_yready)
                E_s.wait(E_pe, wo_done[c0])
                E_s.wait(E_v, za_pair)
                sg = SGN[:, cx * 256:(cx + 1) * 256]
                ybk = pb[bk][:, sub * 256:(sub + 1) * 256]
                sgn_cnt[cx] = E_s.op(
                    lambda e, ybk=ybk, sg=sg: e.activation(sg, ybk, AF.Sign,
                                                           bias=CONSTC[:, 0:1]),
                    reads=(f"pb{bk}", "CONSTC"), writes=("SGN",), inc=True)
            E_v.wait(E_s, sgn_cnt[1])
            for (bk, sub), cx in ctxs:
                MB = MEM[:, cx * FH + 256:cx * FH + 512]
                ybk = pb[bk][:, sub * 256:(sub + 1) * 256]
                E_v.op(lambda e, ybk=ybk, MB=MB: e.tensor_scalar(
                    out=MB, in0=ybk, scalar1=1.9, scalar2=4.5,
                    op0=ALU.mult, op1=ALU.add),
                    reads=(f"pb{bk}",), writes=("MEM",))
            E_pe.wait(E_s, sgn_cnt[0])
            E_pe.wait(E_s, sgn_cnt[1])
            E_pe.wait(E_v, za_pair)
            acc_c = mm(pb[6][:, 0:FH], identb[:], SGN[:, 0:FH],
                       True, False, reads=("identb", "SGN"), writes=("pb6",),
                       inc=True)
            for n in range(2, 6):
                mema_cnt = {}
                for (bk, sub), cx in ctxs:
                    MA = MEM[:, cx * FH:cx * FH + 256]
                    MB = MEM[:, cx * FH + 256:cx * FH + 512]
                    sg = SGN[:, cx * 256:(cx + 1) * 256]
                    ybk = pb[bk][:, sub * 256:(sub + 1) * 256]
                    E_v.wait(E_s, sgn_cnt[cx])
                    mema_cnt[cx] = E_v.op(
                        lambda e, sg=sg, MA=MA, MB=MB: e.scalar_tensor_tensor(
                            out=MA, in0=sg, scalar=-0.5, in1=MB,
                            op0=ALU.mult, op1=ALU.add),
                        reads=("SGN", "MEM"), writes=("MEM",), inc=True)
                    if n < 5:
                        E_v.op(lambda e, MA=MA, MB=MB, ybk=ybk:
                               e.scalar_tensor_tensor(
                                   out=MB, in0=MA, scalar=0.9, in1=ybk,
                                   op0=ALU.mult, op1=ALU.add),
                               reads=("MEM", f"pb{bk}"), writes=("MEM",))
                for (bk, sub), cx in ctxs:
                    MA = MEM[:, cx * FH:cx * FH + 256]
                    sg = SGN[:, cx * 256:(cx + 1) * 256]
                    E_s.wait(E_v, mema_cnt[cx])
                    E_s.wait(E_pe, acc_c)
                    sgn_cnt[cx] = E_s.op(
                        lambda e, sg=sg, MA=MA: e.activation(sg, MA, AF.Sign,
                                                             bias=CONSTC[:, 1:2]),
                        reads=("MEM", "CONSTC"), writes=("SGN",), inc=True)
                E_pe.wait(E_s, sgn_cnt[0])
                E_pe.wait(E_s, sgn_cnt[1])
                acc_c = mm(pb[6][:, 0:FH], identb[:], SGN[:, 0:FH],
                           False, n == 5, reads=("identb", "SGN"), writes=("pb6",),
                           inc=True)
            thg_cnt = {}
            for (bk, sub), cx in ctxs:
                sg = SGN[:, cx * 256:(cx + 1) * 256]
                E_s.wait(E_pe, acc_c)
                thg_cnt[cx] = E_s.op(
                    lambda e, sg=sg, cx=cx: e.activation(
                        sg, pb[6][:, cx * 256:(cx + 1) * 256], AF.Tanh,
                        scale=0.05, bias=CONSTC[:, 2:3]),
                    reads=("pb6", "CONSTC"), writes=("SGN",), inc=True)
            E_v.wait(E_s, thg_cnt[0])
            E_v.wait(E_s, thg_cnt[1])
            for (bk, sub), cx in ctxs:
                tsub, doh = inv[bk]
                MB = MEM[:, cx * FH + 256:cx * FH + 512]
                sg = SGN[:, cx * 256:(cx + 1) * 256]
                gcol = tsub * D + doh * FH + sub * 256
                E_v.op(lambda e, MB=MB, cx=cx: e.tensor_scalar(
                    out=MB, in0=pb[6][:, cx * 256:(cx + 1) * 256], scalar1=0.05,
                    scalar2=0.25, op0=ALU.mult, op1=ALU.add),
                    reads=("pb6",), writes=("MEM",))
                za_pair = E_v.op(lambda e, sg=sg, MB=MB, gcol=gcol:
                                 e.scalar_tensor_tensor(
                                     out=G[:, gcol:gcol + 256], in0=sg, scalar=1.0,
                                     in1=MB, op0=ALU.add, op1=ALU.mult),
                                 reads=("SGN", "MEM"), writes=("G",), inc=True)
            return za_pair

        def emit_combine(cc, ch):
            if sti + 1 < nst:
                E_gp.wait(E_pe, xntp_done[c0 + 3])
            else:
                E_gp.wait(E_v, combine_done.get(c0 - 1, 0))
            if cc == 1:
                E_gp.wait(E_v, combine_done[c0])
            xc = E_gp.dma(XF[:], x_tok_d[ch], "xc")
            E_v.wait(E_gp, xc, dma=True)
            E_v.wait(E_pe, wo_done[ch])
            E_v.wait(E_sy, store_cnt.get(ch - 1, ("st", 0)), dma=True)
            for doh in range(2):
                E_v.op(lambda e, doh=doh, cc=cc: e.tensor_tensor(
                    out=OUTB[:, doh * FH:(doh + 1) * FH], in0=pb[2 + doh][:, 0:FH],
                    in1=G[:, cc * D + doh * FH:cc * D + (doh + 1) * FH],
                    op=ALU.mult),
                    reads=(f"pb{2 + doh}", "G"), writes=("MEM",))
            for doh in range(2):
                c_comb = E_v.op(lambda e, doh=doh: e.tensor_tensor(
                    out=OUTB[:, doh * FH:(doh + 1) * FH],
                    in0=OUTB[:, doh * FH:(doh + 1) * FH],
                    in1=XF[:, doh * FH:(doh + 1) * FH], op=ALU.add),
                    reads=("MEM", "XF"), writes=("MEM",), inc=(doh == 1))
            combine_done[ch] = c_comb
            E_sy.wait(E_v, c_comb)
            store_cnt[ch] = E_sy.dma(out_d[ch], OUTB[:, 0:D], "st")

        za_pair = spike_pair(0, za_pair)
        za_pair = spike_pair(1, za_pair)
        za_pair = spike_pair(2, za_pair)
        za_pair = spike_pair(3, za_pair)
        yq_free_v = za_pair
        yq_free_s = E_s.count
        emit_combine(0, c0)
        E_pe.wait(E_v, combine_done[c0])
        E_pe.wait(E_v, yb_done[c0 + 1])
        for doh in range(2):
            for p_ in range(4):
                c_wo1 = mm(pb[2 + doh][:, 0:FH], SCRB[:, p_ * P:(p_ + 1) * P],
                           wo[:, p_ * D + doh * FH:p_ * D + (doh + 1) * FH],
                           start=(p_ == 0), stop=(p_ == 3),
                           reads=("SCRB", "wo_sb"), writes=(f"pb{2 + doh}",),
                           inc=(doh == 1 and p_ == 3))
        wo_done[c0 + 1] = c_wo1
        emit_combine(1, c0 + 1)

    E_sy.wait(E_sy, store_cnt[nch - 1], dma=True)

    for E in (E_pe, E_v, E_s, E_sy, E_gp):
        E.sem = nc.alloc_semaphore(f"sem_{E.name}")
        if hasattr(E, "dsem"):
            for key in E.dsem:
                E.dsem[key] = nc.alloc_semaphore(f"q_{E.name}_{key}")

    with nc.Block() as blk:
        @blk.sync
        def _(sync):
            for f in E_sy.prog:
                f(sync)

        @blk.gpsimd
        def _(g):
            for f in E_gp.prog:
                f(g)

        @blk.vector
        def _(v):
            for f in E_v.prog:
                f(v)

        @blk.scalar
        def _(s):
            for f in E_s.prog:
                f(s)

        @blk.tensor
        def _(t):
            for f in E_pe.prog:
                f(t)

    return nc


def _prep_host(inputs, T):
    nch = T // CH
    x = np.ascontiguousarray(inputs["L0_attn_x"])
    B = x.shape[0]

    def wlay(Wt):
        d, f = Wt.shape
        return np.ascontiguousarray(Wt.reshape(d // P, P, f).transpose(1, 0, 2)
                                    .reshape(P, (d // P) * f)).astype(np.float32)

    wq_h = wlay(np.ascontiguousarray(inputs["L0_attn_W_Q"].T))
    wk_h = wlay(np.ascontiguousarray(inputs["L0_attn_W_K"].T))
    wv_h = wlay(np.ascontiguousarray(inputs["L0_attn_W_V"].T))
    wg_h = wlay(np.ascontiguousarray(inputs["L0_attn_W_g_w"].T))
    wo_h = wlay(np.ascontiguousarray(inputs["L0_attn_W_O"].T))
    fc_h = wlay(np.ascontiguousarray(inputs["L0_ffn_W_fc"].T))
    pj_h = wlay(np.ascontiguousarray(inputs["L0_ffn_W_proj"].T))
    bg_h = np.ascontiguousarray(np.asarray(inputs["L0_attn_W_g_b"], np.float32)
                                .reshape(4, P).T)
    dl_h = np.ascontiguousarray(np.asarray(inputs["L0_attn_decay_log"],
                                           np.float32).reshape(1, H))
    tt = np.arange(P)
    consts = {
        "identb": np.eye(P).astype(ml_dtypes.bfloat16),
        "tsc": np.maximum(tt[None, :] - tt[:, None], 0).astype(ml_dtypes.bfloat16),
        "zmask": (tt[None, :] >= tt[:, None]).astype(ml_dtypes.bfloat16),
        "r1": np.broadcast_to(tt[None, :] + 1, (P, P)).astype(ml_dtypes.bfloat16),
        "r127": (127 - tt)[:, None].astype(np.float32),
        "onesr": np.ones((1, P), np.float32),
        "onescb": np.ones((P, 1), ml_dtypes.bfloat16),
    }
    selm = np.zeros((H, FH), ml_dtypes.bfloat16)
    for cfull in range(FH):
        selm[cfull // DKH, cfull] = 1.0
    consts["sel"] = selm

    in_maps = []
    for b in range(B):
        x_tok = np.ascontiguousarray(x[b].reshape(nch, P, D)).astype(np.float32)
        m = {"x_tok": x_tok,
             "wq": wq_h, "wk": wk_h, "wv": wv_h, "wg": wg_h, "wo": wo_h,
             "fc": fc_h, "pj": pj_h, "bg": bg_h, "dl": dl_h}
        m.update(consts)
        in_maps.append(m)
    return in_maps


def kernel(**inputs):
    T = inputs["L0_attn_x"].shape[1]
    B = inputs["L0_attn_x"].shape[0]
    nc = build_program(T)
    in_maps = _prep_host(inputs, T)
    res = run_bass_kernel_spmd(nc, in_maps, core_ids=list(range(B)))
    out = np.stack([res.results[b]["out"].reshape(T, D) for b in range(B)], axis=0)
    return out.astype(np.float32)


# version 04
# speedup vs baseline: 1.0562x; 1.0137x over previous
import numpy as np
import ml_dtypes
import concourse.bass as bass
import concourse.mybir as mybir
from concourse.bass_utils import run_bass_kernel_spmd

F32 = mybir.dt.float32
BF16 = mybir.dt.bfloat16
AF = mybir.ActivationFunctionType
ALU = mybir.AluOpType
AXL = mybir.AxisListType

P = 128
D = 1024
FH = 512
H = 8
DKH = 64
FF = 4096
CH = 128
ST = 2
EPS_LN = 1e-5
EPS_AT = 1e-6


class Eng:
    def __init__(self, name, no_drain=False):
        self.name = name
        self.no_drain = no_drain
        self.prog = []
        self.sem = None
        self.count = 0
        self.dirty_w = set()
        self.dirty_r = set()

    def drain(self):
        self.prog.append(lambda e: e.drain())
        self.dirty_w.clear()
        self.dirty_r.clear()

    def op(self, fn, reads=(), writes=(), inc=False):
        hazard = (not self.no_drain) and (
            any(t in self.dirty_w for t in reads) or
            any((t in self.dirty_w or t in self.dirty_r) for t in writes))
        if hazard:
            self.drain()
        if inc:
            self.count += 1
            me = self

            def run(e, fn=fn):
                fn(e).then_inc(me.sem, 1)
            self.prog.append(run)
        else:
            self.prog.append(lambda e, fn=fn: fn(e))
        self.dirty_r.update(reads)
        self.dirty_w.update(writes)
        return self.count

    def dma(self, dst_ap, src_ap, key="d"):
        if not hasattr(self, "dsem"):
            self.dsem = {}
            self.dcnt = {}
        if key not in self.dcnt:
            self.dcnt[key] = 0
            self.dsem[key] = None
        self.dcnt[key] += 1
        me = self

        def run(e, d=dst_ap, s=src_ap, key=key):
            e.dma_start(out=d, in_=s).then_inc(me.dsem[key], 16)
        self.prog.append(run)
        return (key, self.dcnt[key])

    def wait(self, other, value, dma=False):
        if dma:
            key, n = value
            if n <= 0:
                return
            o = other
            self.prog.append(lambda e: e.wait_ge(o.dsem[key], n * 16))
        else:
            if value <= 0:
                return
            o = other
            self.prog.append(lambda e: e.wait_ge(o.sem, value))


def build_program(T):
    nch = T // CH
    nst = nch // ST
    assert nch % ST == 0 and nch >= 4
    nc = bass.Bass("TRN2", target_bir_lowering=False)

    dt_ = nc.dram_tensor
    x_tok_d = dt_("x_tok", [nch, P, D], F32, kind="ExternalInput").ap()
    wq_d = dt_("wq", [P, 8 * FH], F32, kind="ExternalInput").ap()
    wk_d = dt_("wk", [P, 8 * FH], F32, kind="ExternalInput").ap()
    wv_d = dt_("wv", [P, 8 * FH], F32, kind="ExternalInput").ap()
    wg_d = dt_("wg", [P, 8 * FH], F32, kind="ExternalInput").ap()
    wo_d = dt_("wo", [P, 4 * D], F32, kind="ExternalInput").ap()
    fc_d = dt_("fc", [P, 8 * FF], F32, kind="ExternalInput").ap()
    pj_d = dt_("pj", [P, 32 * D], F32, kind="ExternalInput").ap()
    bg_d = dt_("bg", [P, 4], F32, kind="ExternalInput").ap()
    dl_d = dt_("dl", [1, H], F32, kind="ExternalInput").ap()
    identb_d = dt_("identb", [P, P], BF16, kind="ExternalInput").ap()
    tsc_d = dt_("tsc", [P, P], BF16, kind="ExternalInput").ap()
    zmask_d = dt_("zmask", [P, P], BF16, kind="ExternalInput").ap()
    r1_d = dt_("r1", [P, P], BF16, kind="ExternalInput").ap()
    r127_d = dt_("r127", [P, 1], F32, kind="ExternalInput").ap()
    sel_d = dt_("sel", [H, FH], BF16, kind="ExternalInput").ap()
    onesr_d = dt_("onesr", [1, P], F32, kind="ExternalInput").ap()
    onescb_d = dt_("onescb", [P, 1], BF16, kind="ExternalInput").ap()
    out_d = dt_("out", [nch, P, D], F32, kind="ExternalOutput").ap()

    S_ = nc.alloc_sbuf_tensor
    wq = S_("wq_sb", [P, 8 * FH], BF16)
    wk = S_("wk_sb", [P, 8 * FH], BF16)
    wv = S_("wv_sb", [P, 8 * FH], BF16)
    wg = S_("wg_sb", [P, 8 * FH], BF16)
    wo = S_("wo_sb", [P, 4 * D], BF16)
    fc = S_("fc_sb", [P, 8 * FF], BF16)
    pj = S_("pj_sb", [P, 32 * D], BF16)
    XR = S_("XR", [P, 8 * ST * CH], BF16)
    XT = [S_(f"XT{i}", [P, D], BF16) for i in range(2)]
    XF = S_("XF", [P, D], BF16)
    G = S_("G", [P, ST * D], BF16)
    MEM = S_("MEM", [P, D], F32)
    SGN = S_("SGN", [P, FH], BF16)
    QF = S_("QF", [P, FH], BF16)
    KF = S_("KF", [P, FH], BF16)
    THF = S_("THF", [P, FH], BF16)
    VT = S_("VT", [P, FH], BF16)
    VAUG = S_("VAUG", [P, H * 65], BF16)
    AT = S_("AT", [P, H * P], BF16)
    ZP = S_("ZP", [P, 2 * FH], BF16)
    SCRB = S_("SCRB", [P, FH], BF16)
    H2 = [S_(f"H2_{i}", [P, ST * CH], BF16) for i in range(2)]
    OUTB = MEM
    SSB = S_("SSB", [P, 4 * 65], BF16)
    GDEC = S_("GDEC", [P, 4 * 65], BF16)
    DMASK = S_("DMASK", [P, H * P], BF16)
    GQF = S_("GQF", [P, FH], BF16)
    KGM = S_("KGM", [P, FH], BF16)
    identb = S_("identb_sb", [P, P], BF16)
    tsc = G[:, 0:P]
    zmask = G[:, P:2 * P]
    r1t = G[:, 2 * P:3 * P]
    r127 = S_("r127_sb", [P, 1], F32)
    sel = S_("sel_sb", [H, FH], BF16)
    onesr = S_("onesr_sb", [1, P], F32)
    onescb = S_("onescb_sb", [P, 1], BF16)
    bgh = S_("bgh", [P, 4], F32)
    dls = S_("dls", [1, H], F32)
    lgn = S_("lgn", [1, H], F32)
    NLGB = S_("NLGB", [P, H], F32)
    NLGF = S_("NLGF", [P, 4], F32)
    G128B = S_("G128B", [P, H], F32)
    KGC = S_("KGC", [P, H], F32)
    REC = S_("REC", [P, H], F32)
    RECB = S_("RECB", [P, H], BF16)
    stats = S_("stats", [P, 16], F32)
    CONSTC = S_("CONSTC", [P, 3], F32)
    RECT = SCRB[0:H, 0:P]
    ONE64B = SCRB[:, 0:65]

    PS_ = nc.alloc_psum_tensor
    pb = [PS_(f"pb{i}", [P, FH], F32) for i in range(7)]
    pbT = PS_("pbT", [P, 2 * FH], BF16)

    E_pe = Eng("pe", no_drain=True)
    E_v, E_s, E_sy, E_gp = Eng("v"), Eng("s"), Eng("sy"), Eng("gp")

    def mm(out_ap, lhsT_ap, rhs_ap, start, stop, reads=(), writes=(), inc=False, tp=None):
        def f(e):
            return e.matmul(out_ap, lhsT_ap, rhs_ap, start=start, stop=stop,
                            tile_position=tp)
        return E_pe.op(f, reads=reads, writes=writes, inc=inc)

    for w_sb, w_dr in [(fc, fc_d), (pj, pj_d), (wq, wq_d), (wk, wk_d),
                       (wv, wv_d), (wg, wg_d)]:
        E_gp.dma(w_sb[:], w_dr, "wt")
    E_gp.dma(wo[:], wo_d, "wt")

    for dst, src in [(identb[:], identb_d), (tsc, tsc_d), (zmask, zmask_d),
                     (r1t, r1_d), (r127[:], r127_d), (sel[:], sel_d),
                     (onesr[:], onesr_d), (onescb[:], onescb_d),
                     (bgh[:], bg_d), (dls[:], dl_d)]:
        n_cst = E_sy.dma(dst, src, "cst")

    E_v.wait(E_sy, n_cst, dma=True)
    E_s.wait(E_sy, n_cst, dma=True)

    c = E_s.op(lambda e: e.activation(lgn[:], dls[:], AF.Exp, scale=-1.0),
               reads=("dls",), writes=("lgn",), inc=True)
    E_v.wait(E_s, c)
    c = E_v.op(lambda e: e.tensor_scalar_add(out=lgn[:], in0=lgn[:], scalar1=1.0),
               reads=("lgn",), writes=("lgn",), inc=True)
    E_s.wait(E_v, c)
    c_ln = E_s.op(lambda e: e.activation(lgn[:], lgn[:], AF.Ln),
                  reads=("lgn",), writes=("lgn",), inc=True)
    E_v.op(lambda e: e.tensor_scalar_mul(out=bgh[:], in0=bgh[:], scalar1=0.5),
           reads=("bgh",), writes=("bgh",))
    E_v.op(lambda e: e.memset(CONSTC[:, 0:1], -1.0), writes=("CONSTC",))
    E_v.op(lambda e: e.memset(CONSTC[:, 1:2], -6.0), writes=("CONSTC",))
    E_v.op(lambda e: e.memset(CONSTC[:, 2:3], 0.25), writes=("CONSTC",))
    E_v.op(lambda e: e.memset(ZP[:], 0.0), writes=("ZP",))
    E_v.op(lambda e: e.memset(ONE64B, 1.0), writes=("SCRB",))
    c_ssb0 = E_v.op(lambda e: e.memset(SSB[:], 0.0), writes=("SSB",), inc=True)

    E_pe.wait(E_s, c_ln)
    c = mm(pb[5][0:P, 0:H], onesr[:], lgn[:], True, True,
           reads=("onesr", "lgn"), writes=("pb5",), inc=True)
    E_v.wait(E_pe, c)
    c_nl = E_v.op(lambda e: e.tensor_scalar_mul(out=NLGB[:], in0=pb[5][0:P, 0:H],
                                                scalar1=-1.0),
                  reads=("pb5",), writes=("NLGB",), inc=True)
    E_s.wait(E_v, c_nl)
    c_g128 = E_s.op(lambda e: e.activation(G128B[:], NLGB[:], AF.Exp, scale=128.0),
                    reads=("NLGB",), writes=("G128B",), inc=True)
    for h in range(H):
        c_kgc = E_s.op(lambda e, h=h: e.activation(KGC[:, h:h + 1], r127[:], AF.Exp,
                                                   scale=NLGB[:, h:h + 1]),
                       reads=("r127", "NLGB"), writes=("KGC",), inc=(h == H - 1))
    for h in range(H):
        c_dm = E_s.op(lambda e, h=h: e.activation(DMASK[:, h * P:(h + 1) * P], tsc,
                                                  AF.Exp, scale=NLGB[:, h:h + 1]),
                      reads=("G", "NLGB"), writes=("DMASK",), inc=(h == H - 1))
    for ft in range(4):
        E_v.op(lambda e, ft=ft: e.tensor_copy(out=NLGF[0:DKH, ft:ft + 1],
                                              in_=NLGB[0:DKH, 2 * ft:2 * ft + 1]),
               reads=("NLGB",), writes=("NLGF",))
        c_nlgf = E_v.op(lambda e, ft=ft: e.tensor_copy(
            out=NLGF[DKH:P, ft:ft + 1], in_=NLGB[DKH:P, 2 * ft + 1:2 * ft + 2]),
            reads=("NLGB",), writes=("NLGF",), inc=(ft == 3))
    E_s.wait(E_v, c_nlgf)
    for ft in range(4):
        c_gqf = E_s.op(lambda e, ft=ft: e.activation(GQF[:, ft * P:(ft + 1) * P], r1t,
                                                     AF.Exp, scale=NLGF[:, ft:ft + 1]),
                       reads=("G", "NLGF"), writes=("GQF",), inc=(ft == 3))
    E_v.wait(E_s, c_dm)
    for h in range(H):
        E_v.op(lambda e, h=h: e.tensor_tensor(out=DMASK[:, h * P:(h + 1) * P],
                                              in0=DMASK[:, h * P:(h + 1) * P],
                                              in1=zmask, op=ALU.mult),
               reads=("DMASK", "G"), writes=("DMASK",))
    E_v.wait(E_s, c_kgc)
    for h in range(H):
        E_v.op(lambda e, h=h: e.tensor_scalar_mul(out=KGM[:, h * DKH:(h + 1) * DKH],
                                                  in0=ONE64B[:, 0:DKH],
                                                  scalar1=KGC[:, h:h + 1]),
               reads=("SCRB", "KGC"), writes=("KGM",))
        E_v.op(lambda e, h=h: e.tensor_copy(out=VAUG[:, h * 65 + 64:h * 65 + 65],
                                            in_=KGC[:, h:h + 1]),
               reads=("KGC",), writes=("VAUG",))
    E_v.wait(E_s, c_g128)
    for p_ in range(4):
        for r_ in range(2):
            h = 2 * p_ + r_
            E_v.op(lambda e, p_=p_, r_=r_, h=h: e.tensor_scalar_mul(
                out=GDEC[r_ * DKH:(r_ + 1) * DKH, p_ * 65:(p_ + 1) * 65],
                in0=ONE64B[r_ * DKH:(r_ + 1) * DKH, 0:65],
                scalar1=G128B[r_ * DKH:(r_ + 1) * DKH, h:h + 1]),
                reads=("SCRB", "G128B"), writes=("GDEC",), inc=(p_ == 3 and r_ == 1))

    E_pe.wait(E_gp, ("wt", 7), dma=True)
    E_pe.wait(E_v, c_ssb0)

    xt_cnt = {}
    xnact_done = {}
    xnte_done = {}
    xntp_done = {}
    gp_guard = {"s": 0, "g": 0}

    def load_xt(ch):
        E_gp.wait(E_s, xnact_done.get(ch - 2, 0))
        xt_cnt[ch] = E_gp.dma(XT[ch % 2][:], x_tok_d[ch], f"xt{ch % 2}")

    def emit_A(ch, cc):
        buf = XT[ch % 2]
        key = f"XT{ch % 2}"
        st_ = stats
        E_gp.wait(E_gp, xt_cnt[ch], dma=True)
        E_s.wait(E_gp, xt_cnt[ch], dma=True)
        E_s.wait(E_gp, gp_guard["g"])
        E_s.op(lambda e, b=buf: e.activation(XF[:], b[:], AF.Identity,
                                             accum_out=st_[:, 0:1]),
               reads=(key,), writes=("XF",))
        E_s.op(lambda e, b=buf: e.activation(XF[:, 0:FH], b[:, 0:FH], AF.Square,
                                             accum_out=st_[:, 1:2]),
               reads=(key,), writes=("XF",))
        c_sq = E_s.op(lambda e, b=buf: e.activation(XF[:, 0:FH], b[:, FH:D], AF.Square,
                                                    accum_out=st_[:, 2:3]),
                      reads=(key,), writes=("XF",), inc=True)
        E_gp.wait(E_s, c_sq)
        E_gp.wait(E_s, gp_guard["s"])
        g = E_gp
        g.op(lambda e: e.tensor_scalar_mul(out=st_[:, 12:13], in0=st_[:, 0:1],
                                           scalar1=1.0 / D),
             reads=("stats",), writes=("stats",))
        g.op(lambda e: e.tensor_scalar_mul(out=st_[:, 13:14], in0=st_[:, 0:1],
                                           scalar1=-1.0 / D),
             reads=("stats",), writes=("stats",))
        g.op(lambda e: e.tensor_tensor(out=st_[:, 3:4], in0=st_[:, 1:2],
                                       in1=st_[:, 2:3], op=ALU.add),
             reads=("stats",), writes=("stats",))
        g.op(lambda e: e.tensor_scalar(out=st_[:, 3:4], in0=st_[:, 3:4],
                                       scalar1=1.0 / D, scalar2=EPS_LN,
                                       op0=ALU.mult, op1=ALU.add),
             reads=("stats",), writes=("stats",))
        g.op(lambda e: e.tensor_tensor(out=st_[:, 4:5], in0=st_[:, 12:13],
                                       in1=st_[:, 13:14], op=ALU.mult),
             reads=("stats",), writes=("stats",))
        g.op(lambda e: e.tensor_tensor(out=st_[:, 5:6], in0=st_[:, 4:5],
                                       in1=st_[:, 3:4], op=ALU.add),
             reads=("stats",), writes=("stats",))
        g.op(lambda e: e.tensor_scalar(out=st_[:, 6:7], in0=st_[:, 5:6],
                                       scalar1=-0.5, scalar2=1.5,
                                       op0=ALU.mult, op1=ALU.add),
             reads=("stats",), writes=("stats",))
        for _ in range(3):
            g.op(lambda e: e.tensor_tensor(out=st_[:, 7:8], in0=st_[:, 6:7],
                                           in1=st_[:, 6:7], op=ALU.mult),
                 reads=("stats",), writes=("stats",))
            g.op(lambda e: e.tensor_tensor(out=st_[:, 7:8], in0=st_[:, 7:8],
                                           in1=st_[:, 5:6], op=ALU.mult),
                 reads=("stats",), writes=("stats",))
            g.op(lambda e: e.tensor_scalar(out=st_[:, 7:8], in0=st_[:, 7:8],
                                           scalar1=-0.5, scalar2=1.5,
                                           op0=ALU.mult, op1=ALU.add),
                 reads=("stats",), writes=("stats",))
            g.op(lambda e: e.tensor_tensor(out=st_[:, 6:7], in0=st_[:, 6:7],
                                           in1=st_[:, 7:8], op=ALU.mult),
                 reads=("stats",), writes=("stats",))
        g.op(lambda e: e.tensor_tensor(out=st_[:, 10:11], in0=st_[:, 12:13],
                                       in1=st_[:, 6:7], op=ALU.mult),
             reads=("stats",), writes=("stats",))
        c_neg = g.op(lambda e: e.tensor_scalar_mul(out=st_[:, 11:12],
                                                   in0=st_[:, 10:11], scalar1=-1.0),
                     reads=("stats",), writes=("stats",), inc=True)
        gp_guard["g"] = c_neg
        E_s.wait(E_gp, c_neg)
        c_xn = E_s.op(lambda e, b=buf: e.activation(XF[:], b[:], AF.Identity,
                                                    scale=st_[:, 6:7],
                                                    bias=st_[:, 11:12]),
                      reads=(key, "stats"), writes=("XF",), inc=True)
        xnact_done[ch] = c_xn
        gp_guard["s"] = c_xn
        E_pe.wait(E_s, c_xn)
        for dk in range(8):
            c_T = E_pe.op(lambda e, dk=dk: e.transpose(pbT[:, dk * P:(dk + 1) * P],
                                                       XF[:, dk * P:(dk + 1) * P],
                                                       identb[:]),
                          reads=("XF", "identb"), writes=("pbT",), inc=(dk == 7))
        xntp_done[ch] = c_T
        E_s.wait(E_pe, c_T)
        for dk in range(8):
            col = dk * (ST * CH) + cc * CH
            c_ev = E_s.op(lambda e, dk=dk, col=col: e.activation(
                XR[:, col:col + CH], pbT[:, dk * P:(dk + 1) * P], AF.Identity),
                reads=("pbT",), writes=("XR",), inc=(dk == 7))
        xnte_done[ch] = c_ev

    load_xt(0)
    load_xt(1)
    emit_A(0, 0)
    emit_A(1, 1)

    combine_done = {-1: 0, -2: 0}
    combine_mul = {-1: 0, -2: 0}
    store_cnt = {-1: ("st", 0)}
    c_vtab = E_v.count
    th_done = {}
    vaug_done = {}
    ssb_done = {-1: c_ssb0}
    yb_done = {-1: c_vtab, -2: 0}
    wo_done = {-1: 0}
    prev_xnte = xnte_done[1]
    za_pair = 0
    yq_free_v = 0
    yq_free_s = 0

    for sti in range(nst):
        c0 = sti * ST
        chunks = [c0, c0 + 1]
        for cc, ch in enumerate(chunks):
            qbank, kbank = pb[2], pb[3]
            if cc == 0:
                E_pe.wait(E_v, yq_free_v)
                E_pe.wait(E_s, yq_free_s)
                E_pe.wait(E_v, combine_mul[c0 - 1])
                E_pe.wait(E_s, prev_xnte)
            else:
                E_pe.wait(E_s, th_done[ch - 1])
                E_pe.wait(E_v, vaug_done[ch - 1])
            for ft in range(4):
                for dk in range(8):
                    c_qps = mm(qbank[:, ft * P:(ft + 1) * P],
                               wq[:, dk * FH + ft * P:dk * FH + (ft + 1) * P],
                               XR[:, dk * ST * CH + cc * CH:dk * ST * CH + cc * CH + CH],
                               start=(dk == 0), stop=(dk == 7),
                               reads=("wq_sb", "XR"), writes=("pb2",),
                               inc=(ft == 3 and dk == 7))
            E_s.wait(E_pe, c_qps)
            E_s.wait(E_pe, wo_done.get(ch - 1, 0))
            E_s.wait(E_v, yb_done.get(ch - 1, 0))
            E_s.op(lambda e: e.activation(QF[:], qbank[:, 0:FH], AF.Relu, scale=-1.0),
                   reads=("pb2",), writes=("QF",))
            c_qph = E_s.op(lambda e: e.activation(SCRB[:], QF[:], AF.Exp, scale=-1.0),
                           reads=("QF",), writes=("SCRB",), inc=True)
            E_v.wait(E_pe, c_qps)
            E_v.wait(E_s, c_qph)
            c_qf = E_v.op(lambda e: e.scalar_tensor_tensor(
                out=QF[:], in0=qbank[:, 0:FH], scalar=0.0, in1=SCRB[:],
                op0=ALU.max, op1=ALU.add),
                reads=("pb2", "SCRB"), writes=("QF",), inc=True)
            for ft in range(4):
                for dk in range(8):
                    c_kps = mm(kbank[:, ft * P:(ft + 1) * P],
                               wk[:, dk * FH + ft * P:dk * FH + (ft + 1) * P],
                               XR[:, dk * ST * CH + cc * CH:dk * ST * CH + cc * CH + CH],
                               start=(dk == 0), stop=(dk == 7),
                               reads=("wk_sb", "XR"), writes=("pb3",),
                               inc=(ft == 3 and dk == 7))
            E_s.wait(E_pe, c_kps)
            E_s.wait(E_v, c_qf)
            E_s.op(lambda e: e.activation(KF[:], kbank[:, 0:FH], AF.Relu, scale=-1.0),
                   reads=("pb3",), writes=("KF",))
            c_kph = E_s.op(lambda e: e.activation(SCRB[:], KF[:], AF.Exp, scale=-1.0),
                           reads=("KF",), writes=("SCRB",), inc=True)
            E_v.wait(E_pe, c_kps)
            E_v.wait(E_s, c_kph)
            c_kf = E_v.op(lambda e: e.scalar_tensor_tensor(
                out=KF[:], in0=kbank[:, 0:FH], scalar=0.0, in1=SCRB[:],
                op0=ALU.max, op1=ALU.add),
                reads=("pb3", "SCRB"), writes=("KF",), inc=True)
            E_v.op(lambda e: e.tensor_copy(out=ZP[0:DKH, 0:FH], in_=KF[0:DKH, 0:FH]),
                   reads=("KF",), writes=("ZP",))
            c_kz = E_v.op(lambda e: e.tensor_copy(out=ZP[DKH:P, FH:2 * FH],
                                                  in_=KF[DKH:P, 0:FH]),
                          reads=("KF",), writes=("ZP",), inc=True)
            E_pe.wait(E_s, c_qph)
            E_pe.wait(E_v, c_qf)
            for ft in range(4):
                for dk in range(8):
                    c_gps = mm(qbank[:, ft * P:(ft + 1) * P],
                               wg[:, dk * FH + ft * P:dk * FH + (ft + 1) * P],
                               XR[:, dk * ST * CH + cc * CH:dk * ST * CH + cc * CH + CH],
                               start=(dk == 0), stop=(dk == 7),
                               reads=("wg_sb", "XR"), writes=("pb2",),
                               inc=(ft == 3 and dk == 7))
            E_s.wait(E_pe, c_gps)
            E_s.wait(E_v, yb_done.get(ch - 1, 0))
            for ft in range(4):
                c_th = E_s.op(lambda e, ft=ft: e.activation(
                    THF[:, ft * P:(ft + 1) * P], qbank[:, ft * P:(ft + 1) * P],
                    AF.Tanh, scale=0.5, bias=bgh[:, ft:ft + 1]),
                    reads=("pb2", "bgh"), writes=("THF",), inc=(ft == 3))
            th_done[ch] = c_th
            E_pe.wait(E_s, c_kph)
            E_pe.wait(E_v, c_kf)
            for dk in range(8):
                c_vps = mm(kbank[:, 0:FH],
                           XR[:, dk * ST * CH + cc * CH:dk * ST * CH + cc * CH + CH],
                           wv[:, dk * FH:(dk + 1) * FH],
                           start=(dk == 0), stop=(dk == 7),
                           reads=("wv_sb", "XR"), writes=("pb3",), inc=(dk == 7))
            E_v.wait(E_pe, c_vps)
            E_v.op(lambda e: e.tensor_copy(out=VT[:], in_=kbank[:, 0:FH]),
                   reads=("pb3",), writes=("VT",))
            vaug3 = VAUG[:].rearrange("p (h w) -> p h w", w=65)[:, :, 0:DKH]
            vt3 = VT[:].rearrange("p (h w) -> p h w", w=DKH)
            kg3 = KGM[:].rearrange("p (h w) -> p h w", w=DKH)
            c_vaug = E_v.op(lambda e: e.tensor_tensor(out=vaug3, in0=vt3, in1=kg3,
                                                      op=ALU.mult),
                            reads=("VT", "KGM"), writes=("VAUG",), inc=True)
            vaug_done[ch] = c_vaug
            E_pe.wait(E_v, c_kz)
            E_pe.wait(E_v, ssb_done.get(ch - 1, 0))
            for h in range(H):
                r_, ft = h % 2, h // 2
                abank = pb[h // 4]
                c_amm = mm(abank[:, (h % 4) * P:(h % 4 + 1) * P],
                           ZP[:, r_ * FH + ft * P:r_ * FH + (ft + 1) * P],
                           QF[:, ft * P:(ft + 1) * P],
                           True, True, reads=("ZP", "QF"), writes=(f"pb{h // 4}",),
                           inc=(h == H - 1))
            E_v.wait(E_pe, c_amm)
            E_v.op(lambda e: e.tensor_tensor(out=AT[:, 0:FH], in0=pb[0][:, 0:FH],
                                             in1=DMASK[:, 0:FH], op=ALU.mult),
                   reads=("pb0", "DMASK"), writes=("AT",))
            c_at = E_v.op(lambda e: e.tensor_tensor(out=AT[:, FH:2 * FH],
                                                    in0=pb[1][:, 0:FH],
                                                    in1=DMASK[:, FH:2 * FH],
                                                    op=ALU.mult),
                          reads=("pb1", "DMASK"), writes=("AT",), inc=True)
            E_v.op(lambda e: e.tensor_tensor(out=QF[:], in0=QF[:], in1=GQF[:],
                                             op=ALU.mult),
                   reads=("QF", "GQF"), writes=("QF",))
            E_v.op(lambda e: e.tensor_copy(out=ZP[0:DKH, 0:FH], in_=QF[0:DKH, 0:FH]),
                   reads=("QF",), writes=("ZP",))
            c_qz = E_v.op(lambda e: e.tensor_copy(out=ZP[DKH:P, FH:2 * FH],
                                                  in_=QF[DKH:P, 0:FH]),
                          reads=("QF",), writes=("ZP",), inc=True)
            E_pe.wait(E_v, c_at)
            for ft in range(4):
                c_kt = E_pe.op(lambda e, ft=ft: e.transpose(
                    pbT[:, ft * P:(ft + 1) * P], KF[:, ft * P:(ft + 1) * P],
                    identb[:]),
                    reads=("KF", "identb"), writes=("pbT",), inc=(ft == 3))
            opb = pb[4] if cc == 0 else pb[5]
            opkey = "pb4" if cc == 0 else "pb5"
            E_pe.wait(E_v, c_qz)
            for p_ in range(4):
                for r_ in range(2):
                    h = 2 * p_ + r_
                    mm(opb[r_ * DKH:(r_ + 1) * DKH, p_ * P:(p_ + 1) * P],
                       SSB[:, p_ * 65:p_ * 65 + DKH],
                       ZP[:, r_ * FH + p_ * P:r_ * FH + (p_ + 1) * P],
                       True, False, reads=("SSB", "ZP"), writes=(opkey,),
                       tp=(0, r_ * DKH))
                    mm(opb[r_ * DKH:(r_ + 1) * DKH, p_ * P:(p_ + 1) * P],
                       VT[:, h * DKH:(h + 1) * DKH],
                       AT[:, h * P:(h + 1) * P],
                       False, True, reads=("VT", "AT"), writes=(opkey,),
                       tp=(0, r_ * DKH))
            for h in range(H):
                p_, r_ = h // 2, h % 2
                mm(pb[0][0:P, h:h + 1], AT[:, h * P:(h + 1) * P], onescb[:],
                   True, False, reads=("AT", "onescb"), writes=("pb0",))
                c_omm = mm(pb[0][0:P, h:h + 1],
                           ZP[:, r_ * FH + p_ * P:r_ * FH + (p_ + 1) * P],
                           SSB[:, p_ * 65 + DKH:p_ * 65 + 65],
                           False, True, reads=("ZP", "SSB"), writes=("pb0",),
                           inc=(h == H - 1))
            E_v.wait(E_pe, c_kt)
            c_ktok = E_v.op(lambda e: e.tensor_copy(out=KF[:], in_=pbT[:, 0:FH]),
                            reads=("pbT",), writes=("KF",), inc=True)
            E_v.wait(E_pe, c_omm)
            E_v.op(lambda e: e.tensor_scalar_add(out=REC[:], in0=pb[0][0:P, 0:H],
                                                 scalar1=EPS_AT),
                   reads=("pb0",), writes=("REC",))
            E_v.op(lambda e: e.reciprocal(out=REC[:], in_=REC[:]),
                   reads=("REC",), writes=("REC",))
            c_rec = E_v.op(lambda e: e.tensor_copy(out=RECB[:], in_=REC[:]),
                           reads=("REC",), writes=("RECB",), inc=True)
            E_pe.wait(E_v, c_rec)
            c_rt = E_pe.op(lambda e: e.transpose(pbT[0:H, FH:FH + P], RECB[:],
                                                 identb[:]),
                           reads=("RECB", "identb"), writes=("pbT",), inc=True)
            E_v.wait(E_pe, c_rt)
            c_rte = E_v.op(lambda e: e.tensor_copy(out=RECT, in_=pbT[0:H, FH:FH + P]),
                           reads=("pbT",), writes=("SCRB",), inc=True)
            E_pe.wait(E_v, c_rte)
            for ft in range(4):
                mm(pb[1][:, ft * P:(ft + 1) * P], sel[:, ft * P:(ft + 1) * P],
                   RECT, True, True, reads=("sel", "SCRB"), writes=("pb1",))
            for p_ in range(4):
                for r_ in range(2):
                    h = 2 * p_ + r_
                    ft = h // 2
                    c_stmm = mm(pb[0][r_ * DKH:(r_ + 1) * DKH,
                                      128 + p_ * 65:128 + (p_ + 1) * 65],
                                KF[:, ft * P + r_ * DKH:ft * P + (r_ + 1) * DKH],
                                VAUG[:, h * 65:(h + 1) * 65],
                                True, True, reads=("KF", "VAUG"), writes=("pb0",),
                                tp=(0, r_ * DKH), inc=(h == H - 1))
            ybbuf = SGN if cc == 0 else SCRB
            ybkey = "SGN" if cc == 0 else "SCRB"
            E_v.wait(E_pe, c_stmm)
            E_v.wait(E_s, c_th)
            if cc == 0:
                E_v.wait(E_pe, wo_done.get(ch - 2, 0))
            E_v.op(lambda e, b=opb: e.scalar_tensor_tensor(
                out=QF[:], in0=THF[:], scalar=1.0, in1=b[:, 0:FH],
                op0=ALU.add, op1=ALU.mult),
                reads=("THF", opkey), writes=("QF",))
            c_yb = E_v.op(lambda e, yb_=ybbuf: e.scalar_tensor_tensor(
                out=yb_[:, 0:FH], in0=QF[:], scalar=0.5, in1=pb[1][:, 0:FH],
                op0=ALU.mult, op1=ALU.mult),
                reads=("QF", "pb1"), writes=(ybkey,), inc=True)
            yb_done[ch] = c_yb
            E_v.op(lambda e: e.tensor_tensor(out=SSB[:], in0=SSB[:], in1=GDEC[:],
                                             op=ALU.mult),
                   reads=("SSB", "GDEC"), writes=("SSB",))
            c_ssb = E_v.op(lambda e: e.tensor_tensor(out=SSB[:], in0=SSB[:],
                                                     in1=pb[0][:, 128:388], op=ALU.add),
                           reads=("SSB", "pb0"), writes=("SSB",), inc=True)
            ssb_done[ch] = c_ssb

        relu_cnt = {}
        h2_cnt = {}
        fc_cnt = {}
        pj_cnt = {}
        ymap = {(0, 0): 4, (0, 1): 5, (1, 0): 0, (1, 1): 1}
        th_c1 = c_th
        vaug_c1 = c_vaug

        def emit_fc(ft):
            bank = pb[2 + ft % 2]
            E_pe.wait(E_v, h2_cnt.get(ft - 2, 0))
            if ft == 0:
                E_pe.wait(E_s, th_c1)
                E_pe.wait(E_v, vaug_c1)
            for dk in range(8):
                c_fc = mm(bank[:, 0:ST * CH],
                          fc[:, dk * FF + ft * P:dk * FF + (ft + 1) * P],
                          XR[:, dk * ST * CH:(dk + 1) * ST * CH],
                          start=(dk == 0), stop=(dk == 7),
                          reads=("fc_sb", "XR"), writes=(f"pb{2 + ft % 2}",),
                          inc=(dk == 7))
            fc_cnt[ft] = c_fc
            E_s.wait(E_pe, c_fc)
            E_s.wait(E_pe, pj_cnt.get(ft - 2, 0))
            relu_cnt[ft] = E_s.op(
                lambda e, b=bank, ft=ft: e.activation(H2[ft % 2][:], b[:, 0:ST * CH],
                                                      AF.Relu),
                reads=(f"pb{2 + ft % 2}",), writes=(f"H2_{ft % 2}",), inc=True)
            E_v.wait(E_s, relu_cnt[ft])
            h2_cnt[ft] = E_v.op(
                lambda e, b=bank, ft=ft: e.tensor_tensor(
                    out=H2[ft % 2][:], in0=H2[ft % 2][:], in1=b[:, 0:ST * CH],
                    op=ALU.mult),
                reads=(f"H2_{ft % 2}", f"pb{2 + ft % 2}"),
                writes=(f"H2_{ft % 2}",), inc=True)

        def emit_pj(ft):
            E_pe.wait(E_v, h2_cnt[ft])
            if ft == 0:
                E_pe.wait(E_v, ssb_done[c0 + 1])
            for tsub in range(ST):
                for doh in range(2):
                    bank = pb[ymap[(tsub, doh)]]
                    c_pj = mm(bank[:, 0:FH],
                              H2[ft % 2][:, tsub * CH:(tsub + 1) * CH],
                              pj[:, ft * D + doh * FH:ft * D + (doh + 1) * FH],
                              start=(ft == 0), stop=(ft == 31),
                              reads=(f"H2_{ft % 2}", "pj_sb"),
                              writes=(f"pb{ymap[(tsub, doh)]}",),
                              inc=(tsub == ST - 1 and doh == 1))
            pj_cnt[ft] = c_pj

        emit_fc(0)
        emit_fc(1)
        for ft in range(1, 32):
            emit_pj(ft - 1)
            if ft + 1 < 32:
                emit_fc(ft + 1)
        emit_pj(31)
        c_yready = pj_cnt[31]
        h2_31 = h2_cnt[31]

        if sti + 1 < nst:
            load_xt(c0 + 2)
            load_xt(c0 + 3)
            E_pe.wait(E_v, ssb_done[c0 + 1])
            E_pe.wait(E_v, yb_done[c0 + 1])
            emit_A(c0 + 2, 0)
            emit_A(c0 + 3, 1)
            prev_xnte = xnte_done[c0 + 3]

        E_pe.wait(E_v, yb_done[c0])
        E_pe.wait(E_v, h2_31)
        for doh in range(2):
            for p_ in range(4):
                c_wo0 = mm(pb[2 + doh][:, 0:FH], SGN[:, p_ * P:(p_ + 1) * P],
                           wo[:, p_ * D + doh * FH:p_ * D + (doh + 1) * FH],
                           start=(p_ == 0), stop=(p_ == 3),
                           reads=("SGN", "wo_sb"), writes=(f"pb{2 + doh}",),
                           inc=(doh == 1 and p_ == 3))
        wo_done[c0] = c_wo0

        inv = {v: k for k, v in ymap.items()}
        qlist = []
        for b_ in (4, 5, 0, 1):
            for sub in range(2):
                qlist.append((b_, sub))
        sgn_cnt = {}
        E_v.wait(E_pe, c_yready)
        E_v.wait(E_sy, store_cnt.get(c0 - 1, ("st", 0)), dma=True)

        def emit_pair(pi):
            global_unused = None
            nonlocal_vars = None
            return None

        def spike_pair(pi, za_pair):
            ctxs = [(qlist[2 * pi], 0), (qlist[2 * pi + 1], 1)]
            for (bk, sub), cx in ctxs:
                E_s.wait(E_pe, c_yready)
                E_s.wait(E_pe, wo_done[c0])
                E_s.wait(E_v, za_pair)
                sg = SGN[:, cx * 256:(cx + 1) * 256]
                ybk = pb[bk][:, sub * 256:(sub + 1) * 256]
                sgn_cnt[cx] = E_s.op(
                    lambda e, ybk=ybk, sg=sg: e.activation(sg, ybk, AF.Sign,
                                                           bias=CONSTC[:, 0:1]),
                    reads=(f"pb{bk}", "CONSTC"), writes=("SGN",), inc=True)
            E_v.wait(E_s, sgn_cnt[1])
            for (bk, sub), cx in ctxs:
                MB = MEM[:, cx * FH + 256:cx * FH + 512]
                ybk = pb[bk][:, sub * 256:(sub + 1) * 256]
                E_v.op(lambda e, ybk=ybk, MB=MB: e.tensor_scalar(
                    out=MB, in0=ybk, scalar1=1.9, scalar2=4.5,
                    op0=ALU.mult, op1=ALU.add),
                    reads=(f"pb{bk}",), writes=("MEM",))
            E_pe.wait(E_s, sgn_cnt[0])
            E_pe.wait(E_s, sgn_cnt[1])
            E_pe.wait(E_v, za_pair)
            acc_c = mm(pb[6][:, 0:FH], identb[:], SGN[:, 0:FH],
                       True, False, reads=("identb", "SGN"), writes=("pb6",),
                       inc=True)
            for n in range(2, 6):
                mema_cnt = {}
                for (bk, sub), cx in ctxs:
                    MA = MEM[:, cx * FH:cx * FH + 256]
                    MB = MEM[:, cx * FH + 256:cx * FH + 512]
                    sg = SGN[:, cx * 256:(cx + 1) * 256]
                    ybk = pb[bk][:, sub * 256:(sub + 1) * 256]
                    E_v.wait(E_s, sgn_cnt[cx])
                    mema_cnt[cx] = E_v.op(
                        lambda e, sg=sg, MA=MA, MB=MB: e.scalar_tensor_tensor(
                            out=MA, in0=sg, scalar=-0.5, in1=MB,
                            op0=ALU.mult, op1=ALU.add),
                        reads=("SGN", "MEM"), writes=("MEM",), inc=True)
                    if n < 5:
                        E_v.op(lambda e, MA=MA, MB=MB, ybk=ybk:
                               e.scalar_tensor_tensor(
                                   out=MB, in0=MA, scalar=0.9, in1=ybk,
                                   op0=ALU.mult, op1=ALU.add),
                               reads=("MEM", f"pb{bk}"), writes=("MEM",))
                for (bk, sub), cx in ctxs:
                    MA = MEM[:, cx * FH:cx * FH + 256]
                    sg = SGN[:, cx * 256:(cx + 1) * 256]
                    E_s.wait(E_v, mema_cnt[cx])
                    E_s.wait(E_pe, acc_c)
                    sgn_cnt[cx] = E_s.op(
                        lambda e, sg=sg, MA=MA: e.activation(sg, MA, AF.Sign,
                                                             bias=CONSTC[:, 1:2]),
                        reads=("MEM", "CONSTC"), writes=("SGN",), inc=True)
                E_pe.wait(E_s, sgn_cnt[0])
                E_pe.wait(E_s, sgn_cnt[1])
                acc_c = mm(pb[6][:, 0:FH], identb[:], SGN[:, 0:FH],
                           False, n == 5, reads=("identb", "SGN"), writes=("pb6",),
                           inc=True)
            thg_cnt = {}
            for (bk, sub), cx in ctxs:
                sg = SGN[:, cx * 256:(cx + 1) * 256]
                E_s.wait(E_pe, acc_c)
                thg_cnt[cx] = E_s.op(
                    lambda e, sg=sg, cx=cx: e.activation(
                        sg, pb[6][:, cx * 256:(cx + 1) * 256], AF.Tanh,
                        scale=0.05, bias=CONSTC[:, 2:3]),
                    reads=("pb6", "CONSTC"), writes=("SGN",), inc=True)
            E_v.wait(E_s, thg_cnt[0])
            E_v.wait(E_s, thg_cnt[1])
            for (bk, sub), cx in ctxs:
                tsub, doh = inv[bk]
                MB = MEM[:, cx * FH + 256:cx * FH + 512]
                sg = SGN[:, cx * 256:(cx + 1) * 256]
                gcol = tsub * D + doh * FH + sub * 256
                E_v.op(lambda e, MB=MB, cx=cx: e.tensor_scalar(
                    out=MB, in0=pb[6][:, cx * 256:(cx + 1) * 256], scalar1=0.05,
                    scalar2=0.25, op0=ALU.mult, op1=ALU.add),
                    reads=("pb6",), writes=("MEM",))
                za_pair = E_v.op(lambda e, sg=sg, MB=MB, gcol=gcol:
                                 e.scalar_tensor_tensor(
                                     out=G[:, gcol:gcol + 256], in0=sg, scalar=1.0,
                                     in1=MB, op0=ALU.add, op1=ALU.mult),
                                 reads=("SGN", "MEM"), writes=("G",), inc=True)
            return za_pair

        def emit_combine(cc, ch):
            if sti + 1 < nst:
                E_gp.wait(E_pe, xntp_done[c0 + 3])
            E_gp.wait(E_gp, combine_done.get(ch - 1, 0))
            xc = E_gp.dma(XF[:], x_tok_d[ch], "xc")
            E_v.wait(E_pe, wo_done[ch])
            E_v.wait(E_sy, store_cnt.get(ch - 1, ("st", 0)), dma=True)
            for doh in range(2):
                c_mul = E_v.op(lambda e, doh=doh, cc=cc: e.tensor_tensor(
                    out=OUTB[:, doh * FH:(doh + 1) * FH], in0=pb[2 + doh][:, 0:FH],
                    in1=G[:, cc * D + doh * FH:cc * D + (doh + 1) * FH],
                    op=ALU.mult),
                    reads=(f"pb{2 + doh}", "G"), writes=("MEM",), inc=(doh == 1))
            combine_mul[ch] = c_mul
            E_gp.wait(E_gp, xc, dma=True)
            E_gp.wait(E_v, c_mul)
            for doh in range(2):
                c_comb = E_gp.op(lambda e, doh=doh: e.tensor_tensor(
                    out=OUTB[:, doh * FH:(doh + 1) * FH],
                    in0=OUTB[:, doh * FH:(doh + 1) * FH],
                    in1=XF[:, doh * FH:(doh + 1) * FH], op=ALU.add),
                    reads=("MEM", "XF"), writes=("MEM",), inc=(doh == 1))
            combine_done[ch] = c_comb
            E_sy.wait(E_gp, c_comb)
            store_cnt[ch] = E_sy.dma(out_d[ch], OUTB[:, 0:D], "st")

        za_pair = spike_pair(0, za_pair)
        za_pair = spike_pair(1, za_pair)
        za_pair = spike_pair(2, za_pair)
        za_pair = spike_pair(3, za_pair)
        yq_free_v = za_pair
        yq_free_s = E_s.count
        emit_combine(0, c0)
        E_pe.wait(E_v, combine_mul[c0])
        E_pe.wait(E_v, yb_done[c0 + 1])
        for doh in range(2):
            for p_ in range(4):
                c_wo1 = mm(pb[2 + doh][:, 0:FH], SCRB[:, p_ * P:(p_ + 1) * P],
                           wo[:, p_ * D + doh * FH:p_ * D + (doh + 1) * FH],
                           start=(p_ == 0), stop=(p_ == 3),
                           reads=("SCRB", "wo_sb"), writes=(f"pb{2 + doh}",),
                           inc=(doh == 1 and p_ == 3))
        wo_done[c0 + 1] = c_wo1
        emit_combine(1, c0 + 1)
        gp_guard["g"] = combine_done[c0 + 1]

    E_sy.wait(E_sy, store_cnt[nch - 1], dma=True)

    for E in (E_pe, E_v, E_s, E_sy, E_gp):
        E.sem = nc.alloc_semaphore(f"sem_{E.name}")
        if hasattr(E, "dsem"):
            for key in E.dsem:
                E.dsem[key] = nc.alloc_semaphore(f"q_{E.name}_{key}")

    with nc.Block() as blk:
        @blk.sync
        def _(sync):
            for f in E_sy.prog:
                f(sync)

        @blk.gpsimd
        def _(g):
            for f in E_gp.prog:
                f(g)

        @blk.vector
        def _(v):
            for f in E_v.prog:
                f(v)

        @blk.scalar
        def _(s):
            for f in E_s.prog:
                f(s)

        @blk.tensor
        def _(t):
            for f in E_pe.prog:
                f(t)

    return nc


def _prep_host(inputs, T):
    nch = T // CH
    x = np.ascontiguousarray(inputs["L0_attn_x"])
    B = x.shape[0]

    def wlay(Wt):
        d, f = Wt.shape
        return np.ascontiguousarray(Wt.reshape(d // P, P, f).transpose(1, 0, 2)
                                    .reshape(P, (d // P) * f)).astype(np.float32)

    wq_h = wlay(np.ascontiguousarray(inputs["L0_attn_W_Q"].T))
    wk_h = wlay(np.ascontiguousarray(inputs["L0_attn_W_K"].T))
    wv_h = wlay(np.ascontiguousarray(inputs["L0_attn_W_V"].T))
    wg_h = wlay(np.ascontiguousarray(inputs["L0_attn_W_g_w"].T))
    wo_h = wlay(np.ascontiguousarray(inputs["L0_attn_W_O"].T))
    fc_h = wlay(np.ascontiguousarray(inputs["L0_ffn_W_fc"].T))
    pj_h = wlay(np.ascontiguousarray(inputs["L0_ffn_W_proj"].T))
    bg_h = np.ascontiguousarray(np.asarray(inputs["L0_attn_W_g_b"], np.float32)
                                .reshape(4, P).T)
    dl_h = np.ascontiguousarray(np.asarray(inputs["L0_attn_decay_log"],
                                           np.float32).reshape(1, H))
    tt = np.arange(P)
    consts = {
        "identb": np.eye(P).astype(ml_dtypes.bfloat16),
        "tsc": np.maximum(tt[None, :] - tt[:, None], 0).astype(ml_dtypes.bfloat16),
        "zmask": (tt[None, :] >= tt[:, None]).astype(ml_dtypes.bfloat16),
        "r1": np.broadcast_to(tt[None, :] + 1, (P, P)).astype(ml_dtypes.bfloat16),
        "r127": (127 - tt)[:, None].astype(np.float32),
        "onesr": np.ones((1, P), np.float32),
        "onescb": np.ones((P, 1), ml_dtypes.bfloat16),
    }
    selm = np.zeros((H, FH), ml_dtypes.bfloat16)
    for cfull in range(FH):
        selm[cfull // DKH, cfull] = 1.0
    consts["sel"] = selm

    in_maps = []
    for b in range(B):
        x_tok = np.ascontiguousarray(x[b].reshape(nch, P, D)).astype(np.float32)
        m = {"x_tok": x_tok,
             "wq": wq_h, "wk": wk_h, "wv": wv_h, "wg": wg_h, "wo": wo_h,
             "fc": fc_h, "pj": pj_h, "bg": bg_h, "dl": dl_h}
        m.update(consts)
        in_maps.append(m)
    return in_maps


def kernel(**inputs):
    T = inputs["L0_attn_x"].shape[1]
    B = inputs["L0_attn_x"].shape[0]
    nc = build_program(T)
    in_maps = _prep_host(inputs, T)
    res = run_bass_kernel_spmd(nc, in_maps, core_ids=list(range(B)))
    out = np.stack([res.results[b]["out"].reshape(T, D) for b in range(B)], axis=0)
    return out.astype(np.float32)
